# version 25
# baseline (speedup 1.0000x reference)
import numpy as np
from contextlib import ExitStack
import concourse.bass as bass
import concourse.mybir as mybir
from concourse.bass_utils import run_bass_kernel_spmd

F32 = mybir.dt.float32
BF16 = mybir.dt.bfloat16
AF = mybir.ActivationFunctionType
ALU = mybir.AluOpType
AX = mybir.AxisListType

D, T, H, LR, FF = 1024, 2048, 8, 64, 4096
DC = D // 128
QBS = 512
NQB = T // QBS
KC = T // 128
HCN = FF // 128
LN_EPS = 1e-5
U_CLAMP = 80.0
RACE_CHECK = False
SELF_WAITS = True
SW_SKIP = {"tensor"}

R_KVN, R_KVM, R_KVMT = 0, 512, 528
R_MEAN, R_MS, R_VAR, R_RSTD, R_SR, R_MR = 544, 1056, 1568, 2080, 2592, 3104
P4_U4 = 0
P4_MEAN, P4_MS, P4_V1 = 2048, 2560, 3072
P4_E4T = 3584



class Chan:
    __slots__ = ("h", "n")

    def __init__(self, h):
        self.h = h
        self.n = 0


class Plan:
    ENGS = ("sync", "tensor", "vector", "scalar", "gpsimd")

    def __init__(self, block, own):
        self.block = block
        self.own = own
        self.ops = {e: [] for e in self.ENGS}
        self.hwm = {e: {} for e in self.ENGS}

    def op(self, eng, fn, waits=(), inc=None, inc_k=1):
        ws = {}
        own = self.own.get(eng)
        if SELF_WAITS and own is not None and own.n > 0 and eng not in SW_SKIP:
            ws[id(own)] = (own, own.n)
        for w in waits:
            c, v = (w if isinstance(w, tuple) else (w, w.n))
            if v <= 0:
                continue
            k = id(c)
            if k not in ws or v > ws[k][1]:
                ws[k] = (c, v)
        self.ops[eng].append((fn, list(ws.values()), inc, inc_k))
        if inc is not None:
            inc.n += inc_k

    def flush(self):
        for eng in self.ENGS:
            lst = self.ops[eng]
            if not lst:
                continue
            hw = self.hwm[eng]

            def body(e, lst=lst, hw=hw):
                for fn, ws, inc, k in lst:
                    for c, v in ws:
                        if hw.get(id(c), 0) >= v:
                            continue
                        e.wait_ge(c.h, v)
                        hw[id(c)] = v
                    inst = fn(e)
                    if inc is not None:
                        inst.then_inc(inc.h, k)

            getattr(self.block, eng)(body)
            self.ops[eng] = []


def build_nc():
    nc = bass.Bass("TRN2", target_bir_lowering=False,
                   detect_race_conditions=RACE_CHECK)

    xT = nc.dram_tensor("xT", [D, T], F32, kind="ExternalInput").ap()
    wq = nc.dram_tensor("wq", [2048, D], F32, kind="ExternalInput").ap()
    wkvT = nc.dram_tensor("wkvT", [D, LR], F32, kind="ExternalInput").ap()
    wku = nc.dram_tensor("wku", [2048, LR], F32, kind="ExternalInput").ap()
    wvu = nc.dram_tensor("wvu", [2048, LR], F32, kind="ExternalInput").ap()
    woT = nc.dram_tensor("woT", [2048, D], F32, kind="ExternalInput").ap()
    w1p = nc.dram_tensor("w1p", [HCN, D, 128], F32, kind="ExternalInput").ap()
    w2p = nc.dram_tensor("w2p", [DC, FF, 128], F32, kind="ExternalInput").ap()
    b1c = nc.dram_tensor("b1c", [128, HCN], F32, kind="ExternalInput").ap()
    b2c = nc.dram_tensor("b2c", [128, DC], F32, kind="ExternalInput").ap()
    scl = nc.dram_tensor("scl", [1, 1], F32, kind="ExternalInput").ap()
    outT = nc.dram_tensor("outT", [D, T], F32, kind="ExternalOutput").ap()
    z1d = nc.dram_tensor("z1d", [D, T], F32, kind="Internal").ap()
    w1sd = nc.dram_tensor("w1sd", [HCN, D, 256], BF16, kind="Internal").ap()
    w2sd = nc.dram_tensor("w2sd", [DC, FF, 256], BF16, kind="Internal").ap()

    es = ExitStack()
    es_kv = ExitStack()
    es_p12 = ExitStack()
    es_att = ExitStack()
    es_ffn = ExitStack()

    rows = es.enter_context(nc.sbuf_tensor("rows", [1, 4096], F32))
    idn = es.enter_context(nc.sbuf_tensor("idn", [128, 128], F32))
    ones1r = es.enter_context(nc.sbuf_tensor("ones1r", [1, 128], F32))
    ones1c = es.enter_context(nc.sbuf_tensor("ones1c", [128, 1], F32))
    onespair = es.enter_context(nc.sbuf_tensor("onespair", [128, 2], F32))
    sclcol = es.enter_context(nc.sbuf_tensor("sclcol", [128, 1], F32))
    b1s = es.enter_context(nc.sbuf_tensor("b1s", [128, HCN], F32))
    b2s = es.enter_context(nc.sbuf_tensor("b2s", [128, DC], F32))
    scls = es.enter_context(nc.sbuf_tensor("scls", [1, 1], F32))
    Ubias = es.enter_context(nc.sbuf_tensor("Ubias", [128, H], F32))
    qpm = es.enter_context(nc.sbuf_tensor("qpm", [2, 4], F32))
    ones65 = es.enter_context(nc.sbuf_tensor("ones65", [65, 128], F32))
    rcp65 = es.enter_context(nc.sbuf_tensor("rcp65", [65, 512], F32))
    U8 = es.enter_context(nc.sbuf_tensor("U8", [1, 8], F32))

    def row(off, n):
        return rows[0:1, off:off + n]

    ps0 = es.enter_context(nc.psum_tensor("ps0", [128, 512], F32))
    ps1 = es.enter_context(nc.psum_tensor("ps1", [128, 512], F32))
    psU0 = es.enter_context(nc.psum_tensor("psU0", [65, 512], F32))
    psU1 = es.enter_context(nc.psum_tensor("psU1", [65, 512], F32))
    psY0 = es.enter_context(nc.psum_tensor("psY0", [128, 512], F32))
    psY1 = es.enter_context(nc.psum_tensor("psY1", [128, 512], F32))
    psL0 = es.enter_context(nc.psum_tensor("psL0", [128, 512], F32))
    psL1 = es.enter_context(nc.psum_tensor("psL1", [128, 512], F32))
    psA = [ps0, ps1]
    psL = [psL0, psL1]
    psP = [psY0, psY1, ps0, ps1]
    psU = [psU0, psU1]
    psY = [psY0, psY1]

    kv = es_kv.enter_context(nc.sbuf_tensor("kv", [LR, T], F32, side="right"))
    kvtok = es_kv.enter_context(nc.sbuf_tensor("kvtok", [128, KC * 65], F32, side="right"))
    BT = es_kv.enter_context(nc.sbuf_tensor("BT", [128, 4 * D], F32, side="right"))
    qth_t = es_kv.enter_context(nc.sbuf_tensor("qth_t", [128, 4 * T], BF16, side="right"))
    qtl_t = es_kv.enter_context(nc.sbuf_tensor("qtl_t", [128, 4 * T], BF16, side="right"))

    xTs = es_p12.enter_context(nc.sbuf_tensor("xTs", [128, DC * T], F32, side="right"))
    wst = es_p12.enter_context(nc.sbuf_tensor("wst", [128, 4 * D], F32, side="right"))
    wkus = es_p12.enter_context(nc.sbuf_tensor("wkus", [128, 16 * LR], F32, side="right"))
    wkvTs = es_p12.enter_context(nc.sbuf_tensor("wkvTs", [128, DC * LR], F32, side="right"))
    Asb = es_p12.enter_context(nc.sbuf_tensor("Asb", [128, DC * 512], F32, side="right"))
    sqp2 = es_p12.enter_context(nc.sbuf_tensor("sqp2", [128, 512], F32, side="right"))
    qtf = es_p12.enter_context(nc.sbuf_tensor("qtf", [128, 512], F32, side="right"))
    bstage = qtf
    tmp21 = es_p12.enter_context(nc.sbuf_tensor("tmp21", [2, 1], F32, side="right"))

    sem_names = ["cT", "cV", "cS", "cG", "dO", "sP0", "sWST0", "sWST1", "sBT",
                 "sU8", "sXQ0", "sXQ1", "sUA", "sW10", "sW11",
                 "sW20", "sW21", "sU4", "sT1", "sM",
                 "sW1C0", "sW1C1", "sW2C0", "sW2C1", "sZ1", "sZT0", "sZT1",
                 "sST0", "sST1", "sST2", "sST3", "sSU0", "sSU1", "sSU2", "sSU3",
                 "sW12", "sW13", "sW22", "sW23"]
    sems = {n: Chan(es.enter_context(nc.semaphore(n))) for n in sem_names}
    with nc.Block() as block:
        cT, cV, cS, cG = sems["cT"], sems["cV"], sems["cS"], sems["cG"]
        dO = sems["dO"]
        sP0, sBT, sU8, sUA = sems["sP0"], sems["sBT"], sems["sU8"], sems["sUA"]
        sU4, sT1, sM = sems["sU4"], sems["sT1"], sems["sM"]
        sWST = [sems["sWST0"], sems["sWST1"]]
        sXQ = [sems["sXQ0"], sems["sXQ1"]]
        sZ1 = sems["sZ1"]
        sZT = [sems["sZT0"], sems["sZT1"]]
        sW1 = [sems["sW10"], sems["sW11"], sems["sW12"], sems["sW13"]]
        sW2 = [sems["sW20"], sems["sW21"], sems["sW22"], sems["sW23"]]
        P = Plan(block, {"tensor": cT, "vector": cV, "scalar": cS})

        for c in range(DC):
            P.op("sync", lambda e, c=c: e.dma_start(
                out=xTs[:, c * T:(c + 1) * T], in_=xT[c * 128:(c + 1) * 128, :]),
                inc=sP0, inc_k=16)
        dS_xT = sP0.n
        for c in range(DC):
            P.op("sync", lambda e, c=c: e.dma_start(
                out=wkvTs[:, c * LR:(c + 1) * LR],
                in_=wkvT[c * 128:(c + 1) * 128, :]), inc=sP0, inc_k=16)
        dS_wkvT = sP0.n
        for fc in range(16):
            P.op("sync", lambda e, fc=fc: e.dma_start(
                out=wkus[:, fc * LR:(fc + 1) * LR],
                in_=wku[fc * 128:(fc + 1) * 128, :]), inc=sP0, inc_k=16)
        dS_wku = sP0.n
        P.op("sync", lambda e: e.dma_start(out=b1s[:], in_=b1c), inc=sP0, inc_k=16)
        P.op("sync", lambda e: e.dma_start(out=b2s[:], in_=b2c), inc=sP0, inc_k=16)
        P.op("sync", lambda e: e.dma_start(out=scls[:], in_=scl), inc=sP0, inc_k=16)
        dS_misc = sP0.n
        sP0_all = sP0.n

        P.op("gpsimd", lambda e: e.memset(idn[:], 0.0), inc=cG)
        cG_ms = cG.n
        P.op("gpsimd", lambda e: e.affine_select(
            out=idn[:], in_=idn[:], compare_op=ALU.not_equal, fill=1.0,
            base=0, pattern=[[-1, 128]], channel_multiplier=1),
            waits=[(cG, cG_ms)], inc=cG)
        cG_idn = cG.n
        P.op("vector", lambda e: e.memset(ones1r[:], 1.0))
        P.op("vector", lambda e: e.memset(ones1c[:], 1.0))
        P.op("vector", lambda e: e.memset(onespair[0:64, 0:1], 1.0))
        P.op("vector", lambda e: e.memset(onespair[64:128, 1:2], 1.0))
        P.op("vector", lambda e: e.memset(onespair[0:64, 1:2], 0.0))
        P.op("vector", lambda e: e.memset(onespair[64:128, 0:1], 0.0))
        P.op("vector", lambda e: e.memset(qpm[:], -3.0e38))
        P.op("vector", lambda e: e.memset(ones65[:], 1.0))
        P.op("vector", lambda e: e.memset(
            bass.AP(tensor=kvtok[:].tensor, offset=kvtok[:].offset + 64,
                    ap=[[KC * 65, 128], [65, KC], [1, 1]]), 1.0), inc=cV)
        cV_consts = cV.n

        for tb in range(NQB):
            for c in range(DC):
                P.op("tensor", lambda e, tb=tb, c=c: e.matmul(
                    psA[tb % 2][0:64, :],
                    wkvTs[:, c * LR:(c + 1) * LR],
                    xTs[:, c * T + tb * QBS: c * T + (tb + 1) * QBS],
                    start=(c == 0), stop=(c == DC - 1)),
                    waits=[(sP0, sP0_all), cV],
                    inc=(cT if c == DC - 1 else None))
            P.op("vector", lambda e, tb=tb: e.tensor_copy(
                kv[:, tb * QBS:(tb + 1) * QBS], psA[tb % 2][0:64, :]),
                waits=[cT], inc=cV)
        cV_kv = cV.n
        P.op("vector", lambda e: e.memset(row(R_KVM, 1), -3.0e38), inc=cV)
        for tb in range(NQB):
            P.op("vector", lambda e, tb=tb: e.tensor_mul(
                sqp2[0:64, :], kv[:, tb * QBS:(tb + 1) * QBS],
                kv[:, tb * QBS:(tb + 1) * QBS]), waits=[(cV, cV_kv), cT], inc=cV)
            P.op("tensor", lambda e, tb=tb: e.matmul(
                psA[tb % 2][0:1, :], ones1c[0:64, :], sqp2[0:64, :],
                start=True, stop=True), waits=[cV, cT], inc=cT)
            P.op("vector", lambda e, tb=tb: e.tensor_copy(
                row(R_KVN, QBS), psA[tb % 2][0:1, :]), waits=[cT], inc=cV)
            P.op("vector", lambda e: e.reduce_max(
                row(R_KVMT, 1), row(R_KVN, QBS), axis=AX.X), inc=cV)
            P.op("vector", lambda e: e.tensor_max(
                row(R_KVM, 1), row(R_KVM, 1), row(R_KVMT, 1)), inc=cV)
        for kc in range(KC):
            P.op("tensor", lambda e, kc=kc: e.transpose(
                psA[kc % 2][:, 0:64], kv[:, kc * 128:(kc + 1) * 128],
                idn[0:64, 0:64]),
                waits=[(cG, cG_idn), (cV, cV_kv), cV, cT], inc=cT)
            P.op("vector", lambda e, kc=kc: e.tensor_copy(
                kvtok[:, kc * 65:kc * 65 + 64], psA[kc % 2][:, 0:64]),
                waits=[cT], inc=cV)
        wst_free = [0, 0]
        dS_wst = {}

        def plan_wst(key, src):
            head = key % H
            for j in range(2):
                fc = 2 * head + j
                P.op("sync", lambda e, key=key, j=j, fc=fc, src=src: e.dma_start(
                    out=wst[:, (key % 2) * 2 * D + j * D:(key % 2) * 2 * D + (j + 1) * D],
                    in_=src[fc * 128:(fc + 1) * 128, :]),
                    waits=[(cT, wst_free[key % 2])], inc=sWST[key % 2], inc_k=16)
            dS_wst[key] = sWST[key % 2].n

        plan_wst(0, wq)
        plan_wst(1, wq)
        for h in range(H):
            for c in range(DC):
                for j in range(2):
                    P.op("tensor", lambda e, h=h, c=c, j=j: e.matmul(
                        psA[(h * DC + c) % 2][:, 0:64],
                        wst[:, (h % 2) * 2 * D + j * D + c * 128:
                               (h % 2) * 2 * D + j * D + (c + 1) * 128],
                        wkus[:, (2 * h + j) * LR:(2 * h + j + 1) * LR],
                        start=(j == 0), stop=(j == 1)),
                        waits=[(sWST[h % 2], dS_wst[h]), (sP0, sP0_all), cV],
                        inc=(cT if j == 1 else None))
                P.op("vector", lambda e, h=h, c=c: e.tensor_scalar_mul(
                    Asb[:, c * 512 + h * 64: c * 512 + (h + 1) * 64],
                    psA[(h * DC + c) % 2][:, 0:64], 1.0 / 16.0),
                    waits=[cT], inc=cV)
            wst_free[h % 2] = cT.n
            if h + 2 < H:
                plan_wst(h + 2, wq)
        cV_A = cV.n
        cT_A = cT.n
        sWVU = sems["sXQ1"]
        for fc in range(16):
            P.op("sync", lambda e, fc=fc: e.dma_start(
                out=wkus[:, fc * LR:(fc + 1) * LR],
                in_=wvu[fc * 128:(fc + 1) * 128, :]),
                waits=[(cT, cT_A)], inc=sWVU, inc_k=16)
        dS_wvu = sWVU.n
        dS_bst = 0
        for h in range(2):
            plan_wst(H + h, woT)
        for h in range(H):
            for d2 in range(2):
                for j in range(2):
                    P.op("tensor", lambda e, h=h, d2=d2, j=j: e.matmul(
                        psA[(h * 2 + d2) % 2][0:64, :],
                        wkus[:, (2 * h + j) * LR:(2 * h + j + 1) * LR],
                        wst[:, (h % 2) * 2 * D + j * D + d2 * 512:
                               (h % 2) * 2 * D + j * D + (d2 + 1) * 512],
                        start=(j == 0), stop=(j == 1)),
                        waits=[(sWST[h % 2], dS_wst[H + h]), (sWVU, dS_wvu), cV],
                        inc=(cT if j == 1 else None))
                P.op("vector", lambda e, h=h, d2=d2: e.tensor_copy(
                    bstage[0:64, :], psA[(h * 2 + d2) % 2][0:64, :]),
                    waits=[cT, (sBT, dS_bst)], inc=cV)
                cV_bst = cV.n
                P.op("gpsimd", lambda e, h=h, d2=d2: e.dma_start(
                    out=BT[(h % 2) * 64:(h % 2) * 64 + 64,
                           (h // 2) * D + d2 * 512:(h // 2) * D + (d2 + 1) * 512],
                    in_=bstage[0:64, :]), waits=[(cV, cV_bst)], inc=sBT, inc_k=16)
                dS_bst = sBT.n
            wst_free[h % 2] = cT.n
            if h + 2 < H:
                plan_wst(H + h + 2, woT)
        dS_BT_done2 = sBT.n
        P.op("tensor", lambda e: e.matmul(
            psY0[:, 0:1], ones1r[:], scls[:], start=True, stop=True),
            waits=[(sP0, sP0_all), cV], inc=cT)
        P.op("vector", lambda e: e.tensor_copy(sclcol[:], psY0[:, 0:1]),
             waits=[cT], inc=cV)

        cT_pairsum = 0
        for p in range(4):
            for qi in range(NQB):
                for c in range(DC):
                    P.op("tensor", lambda e, p=p, qi=qi, c=c: e.matmul(
                        psA[(p * NQB + qi) % 2][:],
                        Asb[:, c * 512 + p * 128: c * 512 + (p + 1) * 128],
                        xTs[:, c * T + qi * QBS: c * T + (qi + 1) * QBS],
                        start=(c == 0), stop=(c == DC - 1)),
                        waits=[(cV, cV_A), cV],
                        inc=(cT if c == DC - 1 else None))
                par2 = (p * NQB + qi) % 2
                P.op("vector", lambda e, par2=par2: e.tensor_copy(
                    qtf[:], psA[par2][:]), waits=[cT, (sBT, dS_BT_done2)], inc=cV)
                cV_qt = cV.n
                P.op("vector", lambda e, p=p, qi=qi: e.tensor_copy(
                    qth_t[:, p * T + qi * QBS: p * T + (qi + 1) * QBS],
                    qtf[:]), inc=cV)
                P.op("vector", lambda e, p=p, qi=qi: e.tensor_sub(
                    qtl_t[:, p * T + qi * QBS: p * T + (qi + 1) * QBS],
                    qtf[:],
                    qth_t[:, p * T + qi * QBS: p * T + (qi + 1) * QBS]), inc=cV)
                P.op("vector", lambda e: e.tensor_mul(
                    sqp2[:], qtf[:], qtf[:]),
                    waits=[(cT, cT_pairsum)], inc=cV)
                P.op("tensor", lambda e: e.matmul(
                    psU0[0:2, :], onespair[:], sqp2[:], start=True, stop=True),
                    waits=[cV], inc=cT)
                cT_pairsum = cT.n
                P.op("vector", lambda e: e.reduce_max(
                    tmp21[:], psU0[0:2, :], axis=AX.X), waits=[cT], inc=cV)
                P.op("vector", lambda e, p=p: e.tensor_max(
                    qpm[:, p:p + 1], qpm[:, p:p + 1], tmp21[:]),
                    waits=[(cV, cV_consts)], inc=cV)
        cV_qnorm = cV.n
        for p in range(4):
            P.op("gpsimd", lambda e, p=p: e.dma_start(
                out=U8[0:1, 2 * p:2 * p + 2], in_=qpm[:, p:p + 1]),
                waits=[(cV, cV_qnorm)], inc=sU8, inc_k=16)
        dS_U8 = sU8.n
        P.op("vector", lambda e: e.tensor_scalar(
            U8[:], U8[:], row(R_KVM, 1), 0.5, op0=ALU.add, op1=ALU.mult),
            waits=[(sU8, dS_U8)], inc=cV)
        P.op("vector", lambda e: e.tensor_scalar_min(U8[:], U8[:], U_CLAMP), inc=cV)
        cV_U8 = cV.n
        for h in range(H):
            P.op("tensor", lambda e, h=h: e.matmul(
                psY0[:, 0:1], ones1r[:], U8[0:1, h:h + 1], start=True, stop=True),
                waits=[(cV, cV_U8), cV], inc=cT)
            P.op("vector", lambda e, h=h: e.tensor_scalar_mul(
                Ubias[:, h:h + 1], psY0[:, 0:1], -1.0), waits=[cT], inc=cV)
        cV_Ubias = cV.n
        cT_P2_done = cT.n

        dS_BT_done = sBT.n
        P.flush()
        es_p12.close()
        z1cols = es.enter_context(nc.sbuf_tensor("z1cols", [128, DC * 16], F32))

        et = es_att.enter_context(nc.sbuf_tensor("et", [128, 2 * 512], F32, side="right"))
        u_all = es_att.enter_context(nc.sbuf_tensor("u_all", [128, 4 * 512], F32, side="right"))
        xqb = es_att.enter_context(nc.sbuf_tensor("xqb", [128, DC * 512], F32, side="right"))
        ustage = es_att.enter_context(nc.sbuf_tensor("ustage", [64, 512], F32, side="right"))
        kvAh = es_att.enter_context(nc.sbuf_tensor("kvAh", [128, T], BF16, side="right"))
        kvAl = es_att.enter_context(nc.sbuf_tensor("kvAl", [128, T], BF16, side="right"))
        kvBh = es_att.enter_context(nc.sbuf_tensor("kvBh", [128, T], BF16, side="right"))
        kvBl = es_att.enter_context(nc.sbuf_tensor("kvBl", [128, T], BF16, side="right"))
        sqs = es_att.enter_context(nc.sbuf_tensor("sqs", [128, 2 * 512], F32, side="right"))

        P.op("vector", lambda e: e.memset(kvAh[64:128, :], 0.0), waits=[cT], inc=cV)
        P.op("vector", lambda e: e.memset(kvAl[64:128, :], 0.0), inc=cV)
        P.op("vector", lambda e: e.memset(kvBh[0:64, :], 0.0), inc=cV)
        P.op("vector", lambda e: e.memset(kvBl[0:64, :], 0.0), inc=cV)
        P.op("vector", lambda e: e.tensor_copy(kvAh[0:64, :], kv[:]), inc=cV)
        P.op("vector", lambda e: e.tensor_sub(kvAl[0:64, :], kv[:], kvAh[0:64, :]),
             inc=cV)
        cV_kvA = cV.n
        P.op("gpsimd", lambda e: e.dma_start(out=kvBh[64:128, :], in_=kvAh[0:64, :]),
             waits=[(cV, cV_kvA)], inc=sBT, inc_k=16)
        P.op("gpsimd", lambda e: e.dma_start(out=kvBl[64:128, :], in_=kvAl[0:64, :]),
             inc=sBT, inc_k=16)
        sBT_kv = sBT.n

        dS_xqb = [0] * NQB
        xqb_free = [0]
        dS_uall = [0]
        cT_Bproj_prev = [0]
        cV_uall = {}
        cS_exp = {}
        cT_S = {}
        cT_PV = {}
        cT_bcast = {}
        pending_epi = [None]
        pending_ln1 = [None]
        cV_LN1_prev = [0]

        sZ1_done = [0]

        def plan_xqb(qi):
            for c in range(DC):
                P.op("sync", lambda e, qi=qi, c=c: e.dma_start(
                    out=xqb[:, c * 512:(c + 1) * 512],
                    in_=xT[c * 128:(c + 1) * 128, qi * QBS:(qi + 1) * QBS]),
                    waits=[(cV, xqb_free[0]), (sZ1, sZ1_done[0]),
                           (cT, cT_P2_done)],
                    inc=sXQ[0], inc_k=16)
            dS_xqb[qi] = sXQ[0].n

        def plan_epilogue(hq):
            h = hq % H
            P.op("vector", lambda e, h=h: e.reciprocal(
                rcp65[64:65, :], psU[h % 2][64:65, :]),
                waits=[(cT, cT_PV[(hq, KC - 1)]), (cT, cT_bcast.get(hq - 1, 0))],
                inc=cV)
            cV_rcp = cV.n
            P.op("tensor", lambda e, h=h: e.matmul(
                psY[h % 2][0:64, :], ones65[64:65, 0:64], rcp65[64:65, :],
                start=True, stop=True), waits=[(cV, cV_rcp)], inc=cT)
            cT_bcast[hq] = cT.n
            P.op("vector", lambda e, h=h: e.tensor_copy(
                rcp65[0:64, :], psY[h % 2][0:64, :]),
                waits=[(cT, cT_bcast[hq])], inc=cV)
            P.op("vector", lambda e, h=h: e.tensor_mul(
                ustage[:], psU[h % 2][0:64, :], rcp65[0:64, :]),
                waits=[(sUA, dS_uall[0])], inc=cV)
            cV_uall[hq] = cV.n
            P.op("gpsimd", lambda e, h=h: e.dma_start(
                out=u_all[(h % 2) * 64:(h % 2) * 64 + 64,
                          (h // 2) * 512:(h // 2 + 1) * 512],
                in_=ustage[:]),
                waits=[(cV, cV_uall[hq]), (cT, cT_Bproj_prev[0])],
                inc=sUA, inc_k=16)
            dS_uall[0] = sUA.n

        def plan_LN1(qi, cV_r1):
            cT_ss = 0
            for d in range(DC):
                P.op("scalar", lambda e, d=d: e.activation(
                    out=sqs[:, (d % 2) * 512:(d % 2 + 1) * 512],
                    in_=xqb[:, d * 512:(d + 1) * 512], func=AF.Square),
                    waits=[(cV, cV_r1[d]), (cT, cT_ss)], inc=cS)
                cS_sq = cS.n
                P.op("tensor", lambda e, d=d: e.matmul(
                    psL0[0:1, :], ones1c[:], xqb[:, d * 512:(d + 1) * 512],
                    start=(d == 0), stop=(d == DC - 1)),
                    waits=[(cV, cV_r1[d]), (cV, cV_LN1_prev[0]), cS], inc=cT)
                P.op("tensor", lambda e, d=d: e.matmul(
                    psL1[0:1, :], ones1c[:],
                    sqs[:, (d % 2) * 512:(d % 2 + 1) * 512],
                    start=(d == 0), stop=(d == DC - 1)),
                    waits=[(cS, cS_sq), (cV, cV_LN1_prev[0])], inc=cT)
                cT_ss = cT.n
            cT_stats = cT.n
            P.op("vector", lambda e: e.tensor_scalar_mul(
                row(R_MEAN, 512), psL0[0:1, :], 1.0 / D),
                waits=[(cT, cT_stats)], inc=cV)
            P.op("vector", lambda e: e.tensor_scalar_mul(
                row(R_MS, 512), psL1[0:1, :], 1.0 / D), inc=cV)
            P.op("vector", lambda e: e.tensor_mul(
                row(R_VAR, 512), row(R_MEAN, 512), row(R_MEAN, 512)), inc=cV)
            P.op("vector", lambda e: e.tensor_sub(
                row(R_VAR, 512), row(R_MS, 512), row(R_VAR, 512)), inc=cV)
            P.op("vector", lambda e: e.tensor_scalar_add(
                row(R_VAR, 512), row(R_VAR, 512), LN_EPS), inc=cV)
            cV_iv = cV.n
            P.op("scalar", lambda e: e.activation(
                out=row(R_RSTD, 512), in_=row(R_VAR, 512), func=AF.Ln),
                waits=[(cV, cV_iv)], inc=cS)
            P.op("scalar", lambda e: e.activation(
                out=row(R_SR, 512), in_=row(R_RSTD, 512), func=AF.Exp,
                scale=-0.5), inc=cS)
            cS_rstd = cS.n
            P.op("vector", lambda e: e.tensor_mul(
                row(R_MR, 512), row(R_MEAN, 512), row(R_SR, 512)),
                waits=[(cS, cS_rstd)], inc=cV)
            cV_mr = cV.n
            P.op("tensor", lambda e: e.matmul(
                psL0[:], ones1r[:], row(R_SR, 512), start=True, stop=True),
                waits=[(cS, cS_rstd), cV], inc=cT)
            P.op("tensor", lambda e: e.matmul(
                psL1[:], ones1r[:], row(R_MR, 512), start=True, stop=True),
                waits=[(cV, cV_mr)], inc=cT)
            cT_bc2 = cT.n
            for d in range(DC):
                P.op("vector", lambda e, d=d: e.tensor_mul(
                    xqb[:, d * 512:(d + 1) * 512],
                    xqb[:, d * 512:(d + 1) * 512], psL0[:]),
                    waits=[(cT, cT_bc2)], inc=cV)
                P.op("vector", lambda e, d=d: e.tensor_sub(
                    xqb[:, d * 512:(d + 1) * 512],
                    xqb[:, d * 512:(d + 1) * 512], psL1[:]),
                    inc=cV)
                cV_z1w = cV.n
                P.op("gpsimd", lambda e, d=d, qi=qi: e.dma_start(
                    out=z1d[d * 128:(d + 1) * 128, qi * QBS:(qi + 1) * QBS],
                    in_=xqb[:, d * 512:(d + 1) * 512]),
                    waits=[(cV, cV_z1w)], inc=sZ1, inc_k=16)
            if qi == 0:
                for c in range(DC):
                    P.op("vector", lambda e, c=c: e.tensor_copy(
                        z1cols[:, c * 16:(c + 1) * 16],
                        xqb[:, c * 512 + 1: c * 512 + 17]), inc=cV)
            cV_LN1_prev[0] = cV.n
            xqb_free[0] = cV.n
            sZ1_done[0] = sZ1.n
            if qi + 1 < NQB:
                plan_xqb(qi + 1)

        plan_xqb(0)

        for qi in range(NQB):
            for h in range(H):
                hq = qi * H + h

                def plan_S(kc, hq=hq, h=h, qi=qi):
                    par = kc % 2
                    ww = [(cV, cV_Ubias), (sBT, sBT_kv)]
                    prev = cS_exp.get((hq, kc - 2))
                    if prev is None and hq > 0:
                        prev = cS_exp.get((hq - 1, 14 + par))
                    if prev is not None:
                        ww.append((cS, prev))
                    kvh = kvAh if h % 2 == 0 else kvBh
                    kvl = kvAl if h % 2 == 0 else kvBl
                    qs = (h // 2) * T + qi * QBS
                    P.op("tensor", lambda e, kc=kc, par=par, kvh=kvh, qs=qs: e.matmul(
                        psA[par][:], kvh[:, kc * 128:(kc + 1) * 128],
                        qth_t[:, qs:qs + QBS], start=True, stop=False),
                        waits=ww)
                    P.op("tensor", lambda e, kc=kc, par=par, kvh=kvh, qs=qs: e.matmul(
                        psA[par][:], kvh[:, kc * 128:(kc + 1) * 128],
                        qtl_t[:, qs:qs + QBS], start=False, stop=False))
                    P.op("tensor", lambda e, kc=kc, par=par, kvl=kvl, qs=qs: e.matmul(
                        psA[par][:], kvl[:, kc * 128:(kc + 1) * 128],
                        qth_t[:, qs:qs + QBS], start=False, stop=True), inc=cT)
                    cT_S[(hq, kc)] = cT.n

                def plan_exp(kc, hq=hq, h=h):
                    par = kc % 2
                    ww = [(cT, cT_S[(hq, kc)])]
                    prevpv = cT_PV.get((hq, kc - 2))
                    if prevpv is None and hq > 0:
                        prevpv = cT_PV.get((hq - 1, 14 + par))
                    if prevpv is not None:
                        ww.append((cT, prevpv))
                    P.op("scalar", lambda e, kc=kc, par=par, h=h: e.activation(
                        out=et[:, par * 512:(par + 1) * 512], in_=psA[par][:],
                        func=AF.Exp, bias=Ubias[:, h:h + 1], scale=1.0),
                        waits=ww, inc=cS)
                    cS_exp[(hq, kc)] = cS.n

                def plan_PV(kc, hq=hq, h=h):
                    par = kc % 2
                    ww = [(cS, cS_exp[(hq, kc)])]
                    if kc == 0:
                        if hq >= 2:
                            ww.append((cV, cV_uall[hq - 2]))
                        else:
                            ww.append((cV, cV_Ubias))
                    P.op("tensor", lambda e, kc=kc, par=par, h=h: e.matmul(
                        psU[h % 2][:],
                        kvtok[:, kc * 65:(kc + 1) * 65],
                        et[:, par * 512:(par + 1) * 512],
                        start=(kc == 0), stop=(kc == KC - 1)), waits=ww, inc=cT)
                    cT_PV[(hq, kc)] = cT.n

                plan_S(0)
                plan_exp(0)
                plan_S(1)
                plan_exp(1)
                for kc in range(KC):
                    if kc + 2 < KC:
                        plan_S(kc + 2)
                    plan_PV(kc)
                    if kc + 2 < KC:
                        plan_exp(kc + 2)
                    if kc == 1 and pending_epi[0] is not None:
                        plan_epilogue(pending_epi[0])
                        pending_epi[0] = None
                    if kc == 3 and pending_ln1[0] is not None:
                        pqi, pr1 = pending_ln1[0]
                        plan_LN1(pqi, pr1)
                        pending_ln1[0] = None
                if h == H - 1:
                    plan_epilogue(hq)
                else:
                    pending_epi[0] = hq

            cV_r1 = [0] * DC
            for d in range(DC):
                for lc in range(4):
                    P.op("tensor", lambda e, d=d, lc=lc: e.matmul(
                        psY[d % 2][:],
                        BT[:, lc * D + d * 128: lc * D + (d + 1) * 128],
                        u_all[:, lc * 512:(lc + 1) * 512],
                        start=(lc == 0), stop=(lc == 3)),
                        waits=[(sUA, dS_uall[0]), (sBT, dS_BT_done), cV],
                        inc=(cT if lc == 3 else None))
                P.op("vector", lambda e, d=d: e.tensor_add(
                    xqb[:, d * 512:(d + 1) * 512],
                    psY[d % 2][:],
                    xqb[:, d * 512:(d + 1) * 512]),
                    waits=[cT, (sXQ[0], dS_xqb[qi])], inc=cV)
                cV_r1[d] = cV.n
            cT_Bproj_prev[0] = cT.n
            if qi + 1 < NQB:
                pending_ln1[0] = (qi, cV_r1)
            else:
                plan_LN1(qi, cV_r1)
        cV_z1 = cV.n

        P.flush()
        es_att.close()
        es_kv.close()

        hphi = es_ffn.enter_context(nc.sbuf_tensor("hphi", [128, HCN * 512], BF16, side="right"))
        hplo = es_ffn.enter_context(nc.sbuf_tensor("hplo", [128, HCN * 512], BF16, side="right"))
        htmp = es_ffn.enter_context(nc.sbuf_tensor("htmp", [128, 2 * 512], F32, side="right"))
        hgf = es_ffn.enter_context(nc.sbuf_tensor("hgf", [128, 2 * 512], F32, side="right"))
        w1f = es_ffn.enter_context(nc.sbuf_tensor("w1f", [128, 3 * DC * 128], F32, side="right"))
        whl1 = es_ffn.enter_context(nc.sbuf_tensor("whl1", [128, 4 * DC * 256], BF16, side="right"))
        w2f = es_ffn.enter_context(nc.sbuf_tensor("w2f", [128, 3 * 4 * 128], F32, side="right"))
        whl2 = es_ffn.enter_context(nc.sbuf_tensor("whl2", [128, 4 * 4 * 256], BF16, side="right"))
        r2 = es_ffn.enter_context(nc.sbuf_tensor("r2", [128, DC * 512], F32, side="right"))
        u_bc = es_ffn.enter_context(nc.sbuf_tensor("u_bc", [128, 512], F32, side="right"))
        odd4h = es_ffn.enter_context(nc.sbuf_tensor("odd4h", [128, DC * 4], BF16, side="right"))
        odd4l = es_ffn.enter_context(nc.sbuf_tensor("odd4l", [128, DC * 4], BF16, side="right"))
        E4 = es_ffn.enter_context(nc.sbuf_tensor("E4", [4, DC * 128], F32, side="right"))
        sqf = es_ffn.enter_context(nc.sbuf_tensor("sqf", [128, 512 + 32], F32, side="right"))
        z1s = es_ffn.enter_context(nc.sbuf_tensor("z1s", [128, 2 * 4096], BF16, side="right"))
        z1tb = es_ffn.enter_context(nc.sbuf_tensor("z1tb", [128, 2 * DC * 512], F32, side="right"))
        Mt = sqf[:, 512:516]
        u4t = rcp65[0:4, 0:512]
        r2bf = r2[:].bitcast(BF16)

        for c in range(DC):
            P.op("vector", lambda e, c=c: e.tensor_copy(
                odd4h[:, c * 4:(c + 1) * 4],
                z1cols[:, c * 16 + 9: c * 16 + 16: 2]), waits=[(cV, cV_z1)], inc=cV)
            P.op("vector", lambda e, c=c: e.tensor_sub(
                odd4l[:, c * 4:(c + 1) * 4],
                z1cols[:, c * 16 + 9: c * 16 + 16: 2],
                odd4h[:, c * 4:(c + 1) * 4]), inc=cV)
        cV_odd4 = cV.n
        for i in range(4):
            for c in range(DC):
                P.op("tensor", lambda e, i=i, c=c: e.transpose(
                    psA[c % 2][0:1, 0:128],
                    z1cols[:, c * 16 + 8 + 2 * i: c * 16 + 9 + 2 * i], idn[:]),
                    waits=[(cV, cV_z1), cV, cS], inc=cT)
                P.op("vector", lambda e, c=c: e.tensor_scalar(
                    row(P4_MEAN + c * 128, 128), psA[c % 2][0:1, 0:128],
                    scls[:], None, op0=ALU.mult),
                    waits=[cT], inc=cV)
            cV_e4row = cV.n
            for half in range(2):
                d = i * 2 + half
                P.op("gpsimd", lambda e, half=half: e.dma_start(
                    out=Mt, in_=row(P4_MEAN + half * 512, 512)),
                    waits=[(cV, cV_e4row), (cT, cT.n)], inc=sM, inc_k=16)
                dS_M = sM.n
                P.op("tensor", lambda e: e.transpose(
                    psA[0][0:4, 0:128], Mt, idn[:]),
                    waits=[(sM, dS_M), cV], inc=cT)
                P.op("vector", lambda e, d=d: e.tensor_copy(
                    E4[0:4, d * 128:(d + 1) * 128], psA[0][0:4, 0:128]),
                    waits=[cT], inc=cV)
        cV_E4 = cV.n

        dS_w1 = {}
        dS_w1c = {}
        dS_w2 = {}
        dS_w2c = {}
        cG_w1 = {}
        cG_w2 = {}
        w1_free = [0, 0, 0, 0]
        w2_free = [0, 0, 0, 0]
        dO_tbtot = {}
        dS_u4_prev = 0
        cT_ubc_prev = 0
        cV_norm2_prev2 = [0]
        sW1C = [sems["sW1C0"], sems["sW1C1"]]
        sW2C = [sems["sW2C0"], sems["sW2C1"]]
        sST = [sems["sST0"], sems["sST1"], sems["sST2"], sems["sST3"]]
        sSU = [sems["sSU0"], sems["sSU1"], sems["sSU2"], sems["sSU3"]]
        st1_done = {}
        st2_done = {}
        sST_tot = [None]
        sSU_tot = [None]

        def plan_w1(tb, hc):
            s = hc % 3
            s4 = hc % 4
            if tb > 0:
                for c in range(DC):
                    P.op("sync", lambda e, hc=hc, c=c, s4=s4: e.dma_start(
                        out=whl1[:, s4 * 2048 + c * 256: s4 * 2048 + (c + 1) * 256],
                        in_=w1sd[hc, c * 128:(c + 1) * 128, :]),
                        waits=[(cT, w1_free[s4]), (sST[s4], sST_tot[0][s4])],
                        inc=sW1[s4], inc_k=16)
                dS_w1[(tb, hc)] = sW1[s4].n
                return
            for c in range(DC):
                P.op("sync", lambda e, hc=hc, c=c, s=s: e.dma_start(
                    out=w1f[:, s * DC * 128 + c * 128: s * DC * 128 + (c + 1) * 128],
                    in_=w1p[hc, c * 128:(c + 1) * 128, :]),
                    waits=[(cG, cG_w1.get((tb * HCN + hc) - 3, 0))],
                    inc=sW1[s], inc_k=16)
            dS_w1[(tb, hc)] = sW1[s].n
            for c in range(DC):
                P.op("gpsimd", lambda e, hc=hc, c=c, s4=s4: e.dma_start(
                    out=whl1[:, s4 * 2048 + c * 128: s4 * 2048 + (c + 1) * 128],
                    in_=w1p[hc, c * 128:(c + 1) * 128, :]),
                    waits=[(cT, w1_free[s4]), (sST[s4], st1_done.get(s4, 0))],
                    inc=sW1C[s % 2], inc_k=16)
            dS_w1c[(tb, hc)] = sW1C[s % 2].n
            P.op("gpsimd", lambda e, s=s, s4=s4: e.tensor_sub(
                whl1[:, s4 * 2048 + 1024: s4 * 2048 + 2048],
                w1f[:, s * DC * 128:(s + 1) * DC * 128],
                whl1[:, s4 * 2048: s4 * 2048 + 1024]),
                waits=[(sW1C[s % 2], dS_w1c[(tb, hc)]),
                       (sW1[s], dS_w1[(tb, hc)]), (cT, w1_free[s4])], inc=cG)
            cG_w1[tb * HCN + hc] = cG.n

        def plan_w1_store(hc):
            s4 = hc % 4
            cGv = cG_w1[hc]
            for c in range(DC):
                P.op("scalar", lambda e, hc=hc, c=c, s4=s4: e.dma_start(
                    out=w1sd[hc, c * 128:(c + 1) * 128, 0:128],
                    in_=whl1[:, s4 * 2048 + c * 128: s4 * 2048 + (c + 1) * 128]),
                    waits=[(cG, cGv)], inc=sST[s4], inc_k=16)
                P.op("scalar", lambda e, hc=hc, c=c, s4=s4: e.dma_start(
                    out=w1sd[hc, c * 128:(c + 1) * 128, 128:256],
                    in_=whl1[:, s4 * 2048 + 1024 + c * 128:
                             s4 * 2048 + 1024 + (c + 1) * 128]),
                    inc=sST[s4], inc_k=16)
            st1_done[s4] = sST[s4].n

        def plan_w2(tb, d, g):
            gi = d * 8 + g
            s = gi % 3
            s4 = gi % 4
            if tb > 0:
                for k in range(4):
                    P.op("sync", lambda e, d=d, g=g, k=k, s4=s4: e.dma_start(
                        out=whl2[:, s4 * 1024 + k * 256: s4 * 1024 + (k + 1) * 256],
                        in_=w2sd[d, (g * 4 + k) * 128:(g * 4 + k + 1) * 128, :]),
                        waits=[(cT, w2_free[s4]), (sSU[s4], sSU_tot[0][s4])],
                        inc=sW2[s4], inc_k=16)
                dS_w2[(tb, d, g)] = sW2[s4].n
                return
            for k in range(4):
                P.op("sync", lambda e, d=d, g=g, k=k, s=s: e.dma_start(
                    out=w2f[:, s * 4 * 128 + k * 128: s * 4 * 128 + (k + 1) * 128],
                    in_=w2p[d, (g * 4 + k) * 128:(g * 4 + k + 1) * 128, :]),
                    waits=[(cG, cG_w2.get((tb * 64 + d * 8 + g) - 3, 0))],
                    inc=sW2[s], inc_k=16)
            dS_w2[(tb, d, g)] = sW2[s].n
            for k in range(4):
                P.op("gpsimd", lambda e, d=d, g=g, k=k, s4=s4: e.dma_start(
                    out=whl2[:, s4 * 1024 + k * 128: s4 * 1024 + (k + 1) * 128],
                    in_=w2p[d, (g * 4 + k) * 128:(g * 4 + k + 1) * 128, :]),
                    waits=[(cT, w2_free[s4]), (sSU[s4], st2_done.get(s4, 0))],
                    inc=sW2C[s % 2], inc_k=16)
            dS_w2c[(tb, d, g)] = sW2C[s % 2].n
            P.op("gpsimd", lambda e, s=s, s4=s4: e.tensor_sub(
                whl2[:, s4 * 1024 + 512: s4 * 1024 + 1024],
                w2f[:, s * 512:(s + 1) * 512],
                whl2[:, s4 * 1024: s4 * 1024 + 512]),
                waits=[(sW2C[s % 2], dS_w2c[(tb, d, g)]),
                       (sW2[s], dS_w2[(tb, d, g)]), (cT, w2_free[s4])], inc=cG)
            cG_w2[tb * 64 + d * 8 + g] = cG.n

        def plan_w2_store(d, g):
            gi = d * 8 + g
            s4 = gi % 4
            cGv = cG_w2[gi]
            for k in range(4):
                P.op("scalar", lambda e, d=d, g=g, k=k, s4=s4: e.dma_start(
                    out=w2sd[d, (g * 4 + k) * 128:(g * 4 + k + 1) * 128, 0:128],
                    in_=whl2[:, s4 * 1024 + k * 128: s4 * 1024 + (k + 1) * 128]),
                    waits=[(cG, cGv)], inc=sSU[s4], inc_k=16)
                P.op("scalar", lambda e, d=d, g=g, k=k, s4=s4: e.dma_start(
                    out=w2sd[d, (g * 4 + k) * 128:(g * 4 + k + 1) * 128, 128:256],
                    in_=whl2[:, s4 * 1024 + 512 + k * 128:
                             s4 * 1024 + 512 + (k + 1) * 128]),
                    inc=sSU[s4], inc_k=16)
            st2_done[s4] = sSU[s4].n

        def plan_LN2(tb, cV_r2):
            cT_ss2 = 0
            for d in range(DC):
                P.op("scalar", lambda e, d=d: e.activation(
                    out=sqf[:, 0:512],
                    in_=r2[:, d * 512:(d + 1) * 512], func=AF.Square),
                    waits=[(cV, cV_r2[d]), (cT, cT_ss2)], inc=cS)
                cS_sq2 = cS.n
                P.op("tensor", lambda e, d=d: e.matmul(
                    psL0[0:1, :], ones1c[:], r2[:, d * 512:(d + 1) * 512],
                    start=(d == 0), stop=(d == DC - 1)),
                    waits=[(cV, cV_r2[d]), (cV, cV_norm2_prev2[0]), cS, cV], inc=cT)
                P.op("tensor", lambda e, d=d: e.matmul(
                    psL1[0:1, :], ones1c[:], sqf[:, 0:512],
                    start=(d == 0), stop=(d == DC - 1)),
                    waits=[(cS, cS_sq2), (cV, cV_norm2_prev2[0]), cV], inc=cT)
                cT_ss2 = cT.n
            cT_st2 = cT.n
            P.op("vector", lambda e: e.tensor_scalar_mul(
                row(P4_MEAN, 512), psL0[0:1, :], 1.0 / D),
                waits=[(cT, cT_st2)], inc=cV)
            P.op("vector", lambda e: e.tensor_scalar_mul(
                row(P4_MS, 512), psL1[0:1, :], 1.0 / D), inc=cV)
            P.op("vector", lambda e: e.tensor_mul(
                row(P4_V1, 512), row(P4_MEAN, 512), row(P4_MEAN, 512)), inc=cV)
            P.op("vector", lambda e: e.tensor_sub(
                row(P4_MS, 512), row(P4_MS, 512), row(P4_V1, 512)), inc=cV)
            P.op("vector", lambda e: e.tensor_scalar_add(
                row(P4_MS, 512), row(P4_MS, 512), LN_EPS), inc=cV)
            cV_iv2 = cV.n
            P.op("scalar", lambda e: e.activation(
                out=row(P4_V1, 512), in_=row(P4_MS, 512), func=AF.Ln),
                waits=[(cV, cV_iv2)], inc=cS)
            P.op("scalar", lambda e: e.activation(
                out=row(P4_MS, 512), in_=row(P4_V1, 512), func=AF.Exp,
                scale=-0.5), inc=cS)
            cS_rstd2 = cS.n
            P.op("vector", lambda e: e.tensor_mul(
                row(P4_V1, 512), row(P4_MEAN, 512), row(P4_MS, 512)),
                waits=[(cS, cS_rstd2)], inc=cV)
            cV_mr2 = cV.n
            P.op("tensor", lambda e: e.matmul(
                psL0[:], ones1r[:], row(P4_MS, 512), start=True, stop=True),
                waits=[(cS, cS_rstd2), cV], inc=cT)
            P.op("tensor", lambda e: e.matmul(
                psL1[:], ones1r[:], row(P4_V1, 512), start=True, stop=True),
                waits=[(cV, cV_mr2)], inc=cT)
            cT_bc3 = cT.n
            for d in range(DC):
                P.op("vector", lambda e, d=d: e.tensor_mul(
                    r2[:, d * 512:(d + 1) * 512],
                    r2[:, d * 512:(d + 1) * 512], psL0[:]),
                    waits=[(cT, cT_bc3)], inc=cV)
                P.op("vector", lambda e, d=d: e.tensor_sub(
                    r2[:, d * 512:(d + 1) * 512],
                    r2[:, d * 512:(d + 1) * 512], psL1[:]), inc=cV)
                cV_z2 = cV.n
                P.op("scalar", lambda e, d=d, tb=tb: e.dma_start(
                    out=outT[d * 128:(d + 1) * 128, tb * QBS:(tb + 1) * QBS],
                    in_=r2[:, d * 512:(d + 1) * 512]),
                    waits=[(cV, cV_z2)], inc=dO, inc_k=16)
            cV_norm2_prev2[0] = cV.n
            dO_tbtot[tb] = dO.n

        cT_w1_last = [0]
        pending_ln2 = [None]
        z1tb_free = [0, 0]
        dS_z1tb = {}

        def plan_z1tb(tb):
            par = tb % 2
            for c in range(DC):
                P.op("sync", lambda e, tb=tb, c=c, par=par: e.dma_start(
                    out=z1tb[:, par * DC * 512 + c * 512:
                             par * DC * 512 + (c + 1) * 512],
                    in_=z1d[c * 128:(c + 1) * 128, tb * QBS:(tb + 1) * QBS]),
                    waits=[(cV, z1tb_free[par]), (sZ1, sZ1_done[0])],
                    inc=sZT[par], inc_k=16)
            dS_z1tb[tb] = sZT[par].n

        plan_z1tb(0)
        plan_z1tb(1)

        def z1t(tb, c):
            return z1tb[:, (tb % 2) * DC * 512 + c * 512:
                        (tb % 2) * DC * 512 + (c + 1) * 512]

        for tb in range(NQB):
            for c in range(DC):
                P.op("vector", lambda e, c=c, tb=tb: e.tensor_copy(
                    z1s[:, c * 512:(c + 1) * 512], z1t(tb, c)),
                    waits=[(cT, cT_w1_last[0]), (cV, cV_E4),
                           (sZT[tb % 2], dS_z1tb[tb])], inc=cV)
                P.op("vector", lambda e, c=c, tb=tb: e.tensor_sub(
                    z1s[:, 4096 + c * 512: 4096 + (c + 1) * 512],
                    z1t(tb, c),
                    z1s[:, c * 512:(c + 1) * 512]), inc=cV)
            cV_z1s = cV.n

            for c in range(DC):
                P.op("tensor", lambda e, c=c, tb=tb: e.matmul(
                    psL0[0:4, :],
                    z1cols[:, c * 16 + 1: c * 16 + 9: 2],
                    z1t(tb, c),
                    start=(c == 0), stop=(c == DC - 1)),
                    waits=[(cV, cV_E4), (cV, cV_norm2_prev2[0]), cV,
                           (sZT[tb % 2], dS_z1tb[tb])],
                    inc=(cT if c == DC - 1 else None))
            P.op("vector", lambda e: e.tensor_scalar(
                u4t, psL0[0:4, :], sclcol[0:4, :], None, op0=ALU.mult),
                waits=[cT, (sU4, dS_u4_prev)], inc=cV)
            cV_u4 = cV.n
            P.op("gpsimd", lambda e: e.dma_start(
                out=row(P4_U4, 2048), in_=u4t),
                waits=[(cV, cV_u4), (cT, cT_ubc_prev)], inc=sU4, inc_k=16)
            dS_u4 = sU4.n
            dS_u4_prev = dS_u4

            if tb == 1 and sST_tot[0] is None:
                sST_tot[0] = [x.n for x in sST]
                sSU_tot[0] = [x.n for x in sSU]
            for pf in range(3 if tb == 0 else 4):
                plan_w1(tb, pf)
            cS_gelu = {}
            cV_sufuse = {}
            cV_hsplit = {}
            cV_ubc = 0
            for hc in range(HCN):
                i = hc // 8
                s = hc % 2
                s4 = hc % 4
                if hc % 8 == 0:
                    P.op("tensor", lambda e, i=i: e.matmul(
                        psL1[:], ones1r[:], row(P4_U4 + i * 512, 512),
                        start=True, stop=True),
                        waits=[(sU4, dS_u4), (cV, cV_norm2_prev2[0]), cV], inc=cT)
                    cT_ubc = cT.n
                    cT_ubc_prev = cT_ubc
                    P.op("vector", lambda e: e.tensor_copy(u_bc[:], psL1[:]),
                         waits=[(cT, cT_ubc)], inc=cV)
                    cV_ubc = cV.n
                for c in range(DC):
                    if tb == 0:
                        hi1 = s4 * 2048 + c * 128
                        lo1 = s4 * 2048 + 1024 + c * 128
                        wwt = (cG, cG_w1[tb * HCN + hc])
                    else:
                        hi1 = s4 * 2048 + c * 256
                        lo1 = s4 * 2048 + c * 256 + 128
                        wwt = (sW1[s4], dS_w1[(tb, hc)])
                    ww = [wwt, (cV, cV_z1s), (cV, cV_ubc)]
                    if c == 0:
                        ww.append((cV, cV_sufuse.get(hc - 4, 0)))
                    P.op("tensor", lambda e, c=c, s4=s4, hi1=hi1: e.matmul(
                        psP[s4][:],
                        whl1[:, hi1: hi1 + 128],
                        z1s[:, c * 512:(c + 1) * 512],
                        start=(c == 0), stop=False), waits=ww)
                    P.op("tensor", lambda e, c=c, s4=s4, hi1=hi1: e.matmul(
                        psP[s4][:],
                        whl1[:, hi1: hi1 + 128],
                        z1s[:, 4096 + c * 512: 4096 + (c + 1) * 512],
                        start=False, stop=False))
                    P.op("tensor", lambda e, c=c, s4=s4, lo1=lo1: e.matmul(
                        psP[s4][:],
                        whl1[:, lo1: lo1 + 128],
                        z1s[:, c * 512:(c + 1) * 512],
                        start=False, stop=(c == DC - 1)),
                        inc=(cT if c == DC - 1 else None))
                cT_w1 = cT.n
                w1_free[s4] = cT.n
                if tb == 0:
                    plan_w1_store(hc)
                adv1 = 3 if tb == 0 else 4
                if hc + adv1 < HCN:
                    plan_w1(tb, hc + adv1)
                if hc == 2 and pending_ln2[0] is not None:
                    ptb, pr2 = pending_ln2[0]
                    plan_LN2(ptb, pr2)
                    pending_ln2[0] = None
                P.op("vector", lambda e, hc=hc, s=s, s4=s4, i=i: e.scalar_tensor_tensor(
                    htmp[:, s * 512:(s + 1) * 512], u_bc[:],
                    z1cols[:, (hc % 8) * 16 + 2 * i: (hc % 8) * 16 + 2 * i + 1],
                    psP[s4][:], op0=ALU.mult, op1=ALU.add),
                    waits=[(cT, cT_w1), (cS, cS_gelu.get(hc - 2, 0))], inc=cV)
                cV_sufuse[hc] = cV.n
                P.op("scalar", lambda e, hc=hc, s=s: e.activation(
                    out=hgf[:, s * 512:(s + 1) * 512],
                    in_=htmp[:, s * 512:(s + 1) * 512],
                    func=AF.Gelu, bias=b1s[:, hc:hc + 1]),
                    waits=[(cV, cV_sufuse[hc]), (cV, cV_hsplit.get(hc - 2, 0))],
                    inc=cS)
                cS_gelu[hc] = cS.n
                P.op("vector", lambda e, hc=hc, s=s: e.tensor_copy(
                    hphi[:, hc * 512:(hc + 1) * 512],
                    hgf[:, s * 512:(s + 1) * 512]),
                    waits=[(cS, cS_gelu[hc])], inc=cV)
                P.op("vector", lambda e, hc=hc, s=s: e.tensor_sub(
                    hplo[:, hc * 512:(hc + 1) * 512],
                    hgf[:, s * 512:(s + 1) * 512],
                    hphi[:, hc * 512:(hc + 1) * 512]), inc=cV)
                cV_hsplit[hc] = cV.n
            cT_w1_last[0] = cT.n

            cV_T1ev = {}
            for j in range(4):
                for c in range(DC):
                    hb = (j * 8 + c) * 512
                    ww = [(cV, cV_hsplit[j * 8 + c]), (cV, cV_odd4)]
                    if c == 0:
                        ww.append((cV, cV_T1ev.get(j - 2, 0)))
                        ww.append((cV, cV_sufuse[30 + (j % 2)]))
                    P.op("tensor", lambda e, c=c, hb=hb, j=j: e.matmul(
                        psA[j % 2][0:4, :], odd4h[:, c * 4:(c + 1) * 4],
                        hphi[:, hb:hb + 512],
                        start=(c == 0), stop=False), waits=ww)
                    P.op("tensor", lambda e, c=c, hb=hb, j=j: e.matmul(
                        psA[j % 2][0:4, :], odd4h[:, c * 4:(c + 1) * 4],
                        hplo[:, hb:hb + 512], start=False, stop=False))
                    P.op("tensor", lambda e, c=c, hb=hb, j=j: e.matmul(
                        psA[j % 2][0:4, :], odd4l[:, c * 4:(c + 1) * 4],
                        hphi[:, hb:hb + 512], start=False, stop=(c == DC - 1)),
                        inc=(cT if c == DC - 1 else None))
                cT_T1j = cT.n
                P.op("vector", lambda e, j=j: e.tensor_copy(
                    r2[0:4, j * 512:(j + 1) * 512], psA[j % 2][0:4, :]),
                    waits=[(cT, cT_T1j), (dO, dO_tbtot.get(tb - 1, 0))], inc=cV)
                cV_T1ev[j] = cV.n
            cV_T1 = cV.n
            for i in range(4):
                sl = r2[i:i + 1, 0:2048]
                srcap = bass.AP(tensor=sl.tensor, offset=sl.offset,
                                ap=[[sl.ap[0][0], 1], [512, 4], [1, 512]])
                P.op("gpsimd", lambda e, i=i, srcap=srcap: e.dma_start(
                    out=r2[0:4, 2048 + i * 512: 2048 + (i + 1) * 512], in_=srcap),
                    waits=[(cV, cV_T1)], inc=sT1, inc_k=16)
            dS_T1 = sT1.n

            cV_r2 = [0] * DC
            for pf in range(3 if tb == 0 else 4):
                plan_w2(tb, 0, pf)
            for d in range(DC):
                for g in range(8):
                    for k in range(4):
                        hc = g * 4 + k
                        s4w = (d * 8 + g) % 4
                        if tb == 0:
                            hi2 = s4w * 1024 + k * 128
                            lo2 = s4w * 1024 + 512 + k * 128
                            wwt = (cG, cG_w2[tb * 64 + d * 8 + g])
                        else:
                            hi2 = s4w * 1024 + k * 256
                            lo2 = s4w * 1024 + k * 256 + 128
                            wwt = (sW2[s4w], dS_w2[(tb, d, g)])
                        ww = [wwt]
                        if g == 0 and k == 0:
                            if d >= 4:
                                ww.append((cV, cV_r2[d - 4]))
                            elif d >= 2:
                                ww.append((cV, cV_T1ev[d]))
                            else:
                                ww.append((cV, cV_sufuse[28 + d]))
                        st = (g == 0 and k == 0)
                        P.op("tensor", lambda e, d=d, hi2=hi2, hc=hc, st=st: e.matmul(
                            psP[d % 4][:],
                            whl2[:, hi2: hi2 + 128],
                            hphi[:, hc * 512:(hc + 1) * 512],
                            start=st, stop=False), waits=ww)
                        P.op("tensor", lambda e, d=d, hi2=hi2, hc=hc: e.matmul(
                            psP[d % 4][:],
                            whl2[:, hi2: hi2 + 128],
                            hplo[:, hc * 512:(hc + 1) * 512],
                            start=False, stop=False))
                        P.op("tensor", lambda e, d=d, lo2=lo2, hc=hc, k=k: e.matmul(
                            psP[d % 4][:],
                            whl2[:, lo2: lo2 + 128],
                            hphi[:, hc * 512:(hc + 1) * 512],
                            start=False, stop=False),
                            inc=(cT if k == 3 else None))
                    w2_free[(d * 8 + g) % 4] = cT.n
                    if tb == 0:
                        plan_w2_store(d, g)
                    sidx = d * 8 + g
                    adv2 = 3 if tb == 0 else 4
                    if sidx + adv2 <= 63:
                        nd, ng = divmod(sidx + adv2, 8)
                        plan_w2(tb, nd, ng)
                P.op("tensor", lambda e, d=d: e.matmul(
                    psP[d % 4][:],
                    E4[0:4, d * 128:(d + 1) * 128],
                    r2[0:4, 2048 + (d // 2) * 512: 2048 + (d // 2 + 1) * 512],
                    start=False, stop=True),
                    waits=[(sT1, dS_T1), (cV, cV_E4)], inc=cT)
                cT_w2 = cT.n
                P.op("vector", lambda e, d=d, tb=tb: e.scalar_tensor_tensor(
                    r2[:, d * 512:(d + 1) * 512], psP[d % 4][:],
                    b2s[:, d:d + 1], z1t(tb, d),
                    op0=ALU.add, op1=ALU.add),
                    waits=[(cT, cT_w2), (sT1, dS_T1),
                           (dO, dO_tbtot.get(tb - 1, 0))], inc=cV)
                cV_r2[d] = cV.n

            z1tb_free[tb % 2] = cV.n
            if tb + 2 < NQB:
                plan_z1tb(tb + 2)
            if tb + 1 < NQB:
                pending_ln2[0] = (tb, cV_r2)
            else:
                plan_LN2(tb, cV_r2)



        P.op("sync", lambda e: e.nop(), waits=[(dO, dO.n)])
        P.flush()
        es_ffn.close()
    es.close()
    return nc


_NC_CACHE = None


def kernel(L0_attn_x, L0_attn_Wq, L0_attn_Wkv_down, L0_attn_Wk_up, L0_attn_Wv_up,
           L0_attn_Wo, L0_ffn_W1, L0_ffn_b1, L0_ffn_W2, L0_ffn_b2, L0_ffn_scale):
    global _NC_CACHE
    x = np.asarray(L0_attn_x, dtype=np.float32)
    B = x.shape[0]
    wq_h = np.ascontiguousarray(np.asarray(L0_attn_Wq, np.float32))
    wkvT_h = np.ascontiguousarray(np.asarray(L0_attn_Wkv_down, np.float32).T)
    wku_h = np.ascontiguousarray(np.asarray(L0_attn_Wk_up, np.float32))
    wvu_h = np.ascontiguousarray(np.asarray(L0_attn_Wv_up, np.float32))
    woT_h = np.ascontiguousarray(np.asarray(L0_attn_Wo, np.float32).T)
    w1T = np.asarray(L0_ffn_W1, np.float32).T
    w1p_h = np.ascontiguousarray(w1T.reshape(D, HCN, 128).transpose(1, 0, 2))
    w2T = np.asarray(L0_ffn_W2, np.float32).T
    w2p_h = np.ascontiguousarray(w2T.reshape(FF, DC, 128).transpose(1, 0, 2))
    b1c_h = np.ascontiguousarray(np.asarray(L0_ffn_b1, np.float32).reshape(HCN, 128).T)
    b2c_h = np.ascontiguousarray(np.asarray(L0_ffn_b2, np.float32).reshape(DC, 128).T)
    scl_h = np.ascontiguousarray(np.asarray(L0_ffn_scale, np.float32).reshape(1, 1))

    if _NC_CACHE is None:
        _NC_CACHE = build_nc()
    nc = _NC_CACHE

    in_maps = [{
        "xT": np.ascontiguousarray(x[b].T),
        "wq": wq_h, "wkvT": wkvT_h, "wku": wku_h, "wvu": wvu_h, "woT": woT_h,
        "w1p": w1p_h, "w2p": w2p_h, "b1c": b1c_h, "b2c": b2c_h, "scl": scl_h,
    } for b in range(B)]
    res = run_bass_kernel_spmd(nc, in_maps, core_ids=list(range(B)))
    out = np.stack([np.asarray(r["outT"]).T for r in res.results], axis=0)
    return np.ascontiguousarray(out, dtype=np.float32)


# version 26
# speedup vs baseline: 1.0304x; 1.0304x over previous
import numpy as np
from contextlib import ExitStack
import concourse.bass as bass
import concourse.mybir as mybir
from concourse.bass_utils import run_bass_kernel_spmd

F32 = mybir.dt.float32
BF16 = mybir.dt.bfloat16
AF = mybir.ActivationFunctionType
ALU = mybir.AluOpType
AX = mybir.AxisListType

D, T, H, LR, FF = 1024, 2048, 8, 64, 4096
DC = D // 128
QBS = 512
NQB = T // QBS
KC = T // 128
HCN = FF // 128
LN_EPS = 1e-5
U_CLAMP = 80.0
RACE_CHECK = False
SELF_WAITS = True
SW_SKIP = {"tensor"}

R_KVN, R_KVM, R_KVMT = 0, 512, 528
R_MEAN, R_MS, R_VAR, R_RSTD, R_SR, R_MR = 544, 1056, 1568, 2080, 2592, 3104
P4_U4 = 0
P4_MEAN, P4_MS, P4_V1 = 2048, 2560, 3072
P4_E4T = 3584



class Chan:
    __slots__ = ("h", "n")

    def __init__(self, h):
        self.h = h
        self.n = 0


class Plan:
    ENGS = ("sync", "tensor", "vector", "scalar", "gpsimd")

    def __init__(self, block, own):
        self.block = block
        self.own = own
        self.ops = {e: [] for e in self.ENGS}
        self.hwm = {e: {} for e in self.ENGS}

    def op(self, eng, fn, waits=(), inc=None, inc_k=1):
        ws = {}
        own = self.own.get(eng)
        if SELF_WAITS and own is not None and own.n > 0 and eng not in SW_SKIP:
            ws[id(own)] = (own, own.n)
        for w in waits:
            c, v = (w if isinstance(w, tuple) else (w, w.n))
            if v <= 0:
                continue
            k = id(c)
            if k not in ws or v > ws[k][1]:
                ws[k] = (c, v)
        self.ops[eng].append((fn, list(ws.values()), inc, inc_k))
        if inc is not None:
            inc.n += inc_k

    def flush(self):
        for eng in self.ENGS:
            lst = self.ops[eng]
            if not lst:
                continue
            hw = self.hwm[eng]

            def body(e, lst=lst, hw=hw):
                for fn, ws, inc, k in lst:
                    for c, v in ws:
                        if hw.get(id(c), 0) >= v:
                            continue
                        e.wait_ge(c.h, v)
                        hw[id(c)] = v
                    inst = fn(e)
                    if inc is not None:
                        inst.then_inc(inc.h, k)

            getattr(self.block, eng)(body)
            self.ops[eng] = []


def build_nc():
    nc = bass.Bass("TRN2", target_bir_lowering=False,
                   detect_race_conditions=RACE_CHECK)

    xT = nc.dram_tensor("xT", [D, T], F32, kind="ExternalInput").ap()
    wq = nc.dram_tensor("wq", [2048, D], F32, kind="ExternalInput").ap()
    wkvT = nc.dram_tensor("wkvT", [D, LR], F32, kind="ExternalInput").ap()
    wku = nc.dram_tensor("wku", [2048, LR], F32, kind="ExternalInput").ap()
    wvu = nc.dram_tensor("wvu", [2048, LR], F32, kind="ExternalInput").ap()
    woT = nc.dram_tensor("woT", [2048, D], F32, kind="ExternalInput").ap()
    w1p = nc.dram_tensor("w1p", [HCN, D, 128], F32, kind="ExternalInput").ap()
    w2p = nc.dram_tensor("w2p", [DC, FF, 128], F32, kind="ExternalInput").ap()
    b1c = nc.dram_tensor("b1c", [128, HCN], F32, kind="ExternalInput").ap()
    b2c = nc.dram_tensor("b2c", [128, DC], F32, kind="ExternalInput").ap()
    scl = nc.dram_tensor("scl", [1, 1], F32, kind="ExternalInput").ap()
    outT = nc.dram_tensor("outT", [D, T], F32, kind="ExternalOutput").ap()
    z1d = nc.dram_tensor("z1d", [D, T], F32, kind="Internal").ap()
    w1sd = nc.dram_tensor("w1sd", [HCN, D, 256], BF16, kind="Internal").ap()
    w2sd = nc.dram_tensor("w2sd", [DC, FF, 256], BF16, kind="Internal").ap()

    es = ExitStack()
    es_kv = ExitStack()
    es_p12 = ExitStack()
    es_att = ExitStack()
    es_ffn = ExitStack()

    rows = es.enter_context(nc.sbuf_tensor("rows", [1, 4096], F32))
    idn = es.enter_context(nc.sbuf_tensor("idn", [128, 128], F32))
    ones1r = es.enter_context(nc.sbuf_tensor("ones1r", [1, 128], F32))
    ones1c = es.enter_context(nc.sbuf_tensor("ones1c", [128, 1], F32))
    onespair = es.enter_context(nc.sbuf_tensor("onespair", [128, 2], F32))
    sclcol = es.enter_context(nc.sbuf_tensor("sclcol", [128, 1], F32))
    b1s = es.enter_context(nc.sbuf_tensor("b1s", [128, HCN], F32))
    b2s = es.enter_context(nc.sbuf_tensor("b2s", [128, DC], F32))
    scls = es.enter_context(nc.sbuf_tensor("scls", [1, 1], F32))
    Ubias = es.enter_context(nc.sbuf_tensor("Ubias", [128, H], F32))
    qpm = es.enter_context(nc.sbuf_tensor("qpm", [2, 4], F32))
    ones65 = es.enter_context(nc.sbuf_tensor("ones65", [65, 128], F32))
    rcp65 = es.enter_context(nc.sbuf_tensor("rcp65", [65, 512], F32))
    U8 = es.enter_context(nc.sbuf_tensor("U8", [1, 8], F32))

    def row(off, n):
        return rows[0:1, off:off + n]

    ps0 = es.enter_context(nc.psum_tensor("ps0", [128, 512], F32))
    ps1 = es.enter_context(nc.psum_tensor("ps1", [128, 512], F32))
    psU0 = es.enter_context(nc.psum_tensor("psU0", [65, 512], F32))
    psU1 = es.enter_context(nc.psum_tensor("psU1", [65, 512], F32))
    psY0 = es.enter_context(nc.psum_tensor("psY0", [128, 512], F32))
    psY1 = es.enter_context(nc.psum_tensor("psY1", [128, 512], F32))
    psL0 = es.enter_context(nc.psum_tensor("psL0", [128, 512], F32))
    psL1 = es.enter_context(nc.psum_tensor("psL1", [128, 512], F32))
    psA = [ps0, ps1]
    psL = [psL0, psL1]
    psP = [psY0, psY1, ps0, ps1]
    psU = [psU0, psU1]
    psY = [psY0, psY1]

    kv = es_kv.enter_context(nc.sbuf_tensor("kv", [LR, T], F32, side="right"))
    kvtok = es_kv.enter_context(nc.sbuf_tensor("kvtok", [128, KC * 65], F32, side="right"))
    BT = es_kv.enter_context(nc.sbuf_tensor("BT", [128, 4 * D], F32, side="right"))
    qth_t = es_kv.enter_context(nc.sbuf_tensor("qth_t", [128, 4 * T], BF16, side="right"))
    qtl_t = es_kv.enter_context(nc.sbuf_tensor("qtl_t", [128, 4 * T], BF16, side="right"))

    xTs = es_p12.enter_context(nc.sbuf_tensor("xTs", [128, DC * T], F32, side="right"))
    wst = es_p12.enter_context(nc.sbuf_tensor("wst", [128, 4 * D], F32, side="right"))
    wkus = es_p12.enter_context(nc.sbuf_tensor("wkus", [128, 16 * LR], F32, side="right"))
    wkvTs = es_p12.enter_context(nc.sbuf_tensor("wkvTs", [128, DC * LR], F32, side="right"))
    Asb = es_p12.enter_context(nc.sbuf_tensor("Asb", [128, DC * 512], F32, side="right"))
    sqp2 = es_p12.enter_context(nc.sbuf_tensor("sqp2", [128, 512], F32, side="right"))
    qtf = es_p12.enter_context(nc.sbuf_tensor("qtf", [128, 512], F32, side="right"))
    bstage = qtf
    tmp21 = es_p12.enter_context(nc.sbuf_tensor("tmp21", [2, 1], F32, side="right"))

    sem_names = ["cT", "cV", "cS", "cG", "dO", "sP0", "sWST0", "sWST1", "sBT",
                 "sU8", "sXQ0", "sXQ1", "sUA", "sW10", "sW11",
                 "sW20", "sW21", "sU4", "sT1", "sM",
                 "sW1C0", "sW1C1", "sW2C0", "sW2C1", "sZ1", "sZT0", "sZT1",
                 "sST0", "sST1", "sST2", "sST3", "sSU0", "sSU1", "sSU2", "sSU3",
                 "sW12", "sW13", "sW22", "sW23"]
    sems = {n: Chan(es.enter_context(nc.semaphore(n))) for n in sem_names}
    with nc.Block() as block:
        cT, cV, cS, cG = sems["cT"], sems["cV"], sems["cS"], sems["cG"]
        dO = sems["dO"]
        sP0, sBT, sU8, sUA = sems["sP0"], sems["sBT"], sems["sU8"], sems["sUA"]
        sU4, sT1, sM = sems["sU4"], sems["sT1"], sems["sM"]
        sWST = [sems["sWST0"], sems["sWST1"]]
        sXQ = [sems["sXQ0"], sems["sXQ1"]]
        sZ1 = sems["sZ1"]
        sZT = [sems["sZT0"], sems["sZT1"]]
        sW1 = [sems["sW10"], sems["sW11"], sems["sW12"], sems["sW13"]]
        sW2 = [sems["sW20"], sems["sW21"], sems["sW22"], sems["sW23"]]
        P = Plan(block, {"tensor": cT, "vector": cV, "scalar": cS})

        for c in range(DC):
            P.op("sync", lambda e, c=c: e.dma_start(
                out=xTs[:, c * T:(c + 1) * T], in_=xT[c * 128:(c + 1) * 128, :]),
                inc=sP0, inc_k=16)
        dS_xT = sP0.n
        for c in range(DC):
            P.op("sync", lambda e, c=c: e.dma_start(
                out=wkvTs[:, c * LR:(c + 1) * LR],
                in_=wkvT[c * 128:(c + 1) * 128, :]), inc=sP0, inc_k=16)
        dS_wkvT = sP0.n
        for fc in range(16):
            P.op("sync", lambda e, fc=fc: e.dma_start(
                out=wkus[:, fc * LR:(fc + 1) * LR],
                in_=wku[fc * 128:(fc + 1) * 128, :]), inc=sP0, inc_k=16)
        dS_wku = sP0.n
        P.op("sync", lambda e: e.dma_start(out=b1s[:], in_=b1c), inc=sP0, inc_k=16)
        P.op("sync", lambda e: e.dma_start(out=b2s[:], in_=b2c), inc=sP0, inc_k=16)
        P.op("sync", lambda e: e.dma_start(out=scls[:], in_=scl), inc=sP0, inc_k=16)
        dS_misc = sP0.n
        sP0_all = sP0.n

        P.op("gpsimd", lambda e: e.memset(idn[:], 0.0), inc=cG)
        cG_ms = cG.n
        P.op("gpsimd", lambda e: e.affine_select(
            out=idn[:], in_=idn[:], compare_op=ALU.not_equal, fill=1.0,
            base=0, pattern=[[-1, 128]], channel_multiplier=1),
            waits=[(cG, cG_ms)], inc=cG)
        cG_idn = cG.n
        P.op("vector", lambda e: e.memset(ones1r[:], 1.0))
        P.op("vector", lambda e: e.memset(ones1c[:], 1.0))
        P.op("vector", lambda e: e.memset(onespair[0:64, 0:1], 1.0))
        P.op("vector", lambda e: e.memset(onespair[64:128, 1:2], 1.0))
        P.op("vector", lambda e: e.memset(onespair[0:64, 1:2], 0.0))
        P.op("vector", lambda e: e.memset(onespair[64:128, 0:1], 0.0))
        P.op("vector", lambda e: e.memset(qpm[:], -3.0e38))
        P.op("vector", lambda e: e.memset(ones65[:], 1.0))
        P.op("vector", lambda e: e.memset(
            bass.AP(tensor=kvtok[:].tensor, offset=kvtok[:].offset + 64,
                    ap=[[KC * 65, 128], [65, KC], [1, 1]]), 1.0), inc=cV)
        cV_consts = cV.n

        for tb in range(NQB):
            for c in range(DC):
                P.op("tensor", lambda e, tb=tb, c=c: e.matmul(
                    psA[tb % 2][0:64, :],
                    wkvTs[:, c * LR:(c + 1) * LR],
                    xTs[:, c * T + tb * QBS: c * T + (tb + 1) * QBS],
                    start=(c == 0), stop=(c == DC - 1)),
                    waits=[(sP0, sP0_all), cV],
                    inc=(cT if c == DC - 1 else None))
            P.op("vector", lambda e, tb=tb: e.tensor_copy(
                kv[:, tb * QBS:(tb + 1) * QBS], psA[tb % 2][0:64, :]),
                waits=[cT], inc=cV)
        cV_kv = cV.n
        P.op("vector", lambda e: e.memset(row(R_KVM, 1), -3.0e38), inc=cV)
        for tb in range(NQB):
            P.op("vector", lambda e, tb=tb: e.tensor_mul(
                sqp2[0:64, :], kv[:, tb * QBS:(tb + 1) * QBS],
                kv[:, tb * QBS:(tb + 1) * QBS]), waits=[(cV, cV_kv), cT], inc=cV)
            P.op("tensor", lambda e, tb=tb: e.matmul(
                psA[tb % 2][0:1, :], ones1c[0:64, :], sqp2[0:64, :],
                start=True, stop=True), waits=[cV, cT], inc=cT)
            P.op("vector", lambda e, tb=tb: e.tensor_copy(
                row(R_KVN, QBS), psA[tb % 2][0:1, :]), waits=[cT], inc=cV)
            P.op("vector", lambda e: e.reduce_max(
                row(R_KVMT, 1), row(R_KVN, QBS), axis=AX.X), inc=cV)
            P.op("vector", lambda e: e.tensor_max(
                row(R_KVM, 1), row(R_KVM, 1), row(R_KVMT, 1)), inc=cV)
        for kc in range(KC):
            P.op("tensor", lambda e, kc=kc: e.transpose(
                psA[kc % 2][:, 0:64], kv[:, kc * 128:(kc + 1) * 128],
                idn[0:64, 0:64]),
                waits=[(cG, cG_idn), (cV, cV_kv), cV, cT], inc=cT)
            P.op("vector", lambda e, kc=kc: e.tensor_copy(
                kvtok[:, kc * 65:kc * 65 + 64], psA[kc % 2][:, 0:64]),
                waits=[cT], inc=cV)
        wst_free = [0, 0]
        dS_wst = {}

        def plan_wst(key, src):
            head = key % H
            for j in range(2):
                fc = 2 * head + j
                P.op("sync", lambda e, key=key, j=j, fc=fc, src=src: e.dma_start(
                    out=wst[:, (key % 2) * 2 * D + j * D:(key % 2) * 2 * D + (j + 1) * D],
                    in_=src[fc * 128:(fc + 1) * 128, :]),
                    waits=[(cT, wst_free[key % 2])], inc=sWST[key % 2], inc_k=16)
            dS_wst[key] = sWST[key % 2].n

        plan_wst(0, wq)
        plan_wst(1, wq)
        for h in range(H):
            for c in range(DC):
                for j in range(2):
                    P.op("tensor", lambda e, h=h, c=c, j=j: e.matmul(
                        psA[(h * DC + c) % 2][:, 0:64],
                        wst[:, (h % 2) * 2 * D + j * D + c * 128:
                               (h % 2) * 2 * D + j * D + (c + 1) * 128],
                        wkus[:, (2 * h + j) * LR:(2 * h + j + 1) * LR],
                        start=(j == 0), stop=(j == 1)),
                        waits=[(sWST[h % 2], dS_wst[h]), (sP0, sP0_all), cV],
                        inc=(cT if j == 1 else None))
                P.op("vector", lambda e, h=h, c=c: e.tensor_scalar_mul(
                    Asb[:, c * 512 + h * 64: c * 512 + (h + 1) * 64],
                    psA[(h * DC + c) % 2][:, 0:64], 1.0 / 16.0),
                    waits=[cT], inc=cV)
            wst_free[h % 2] = cT.n
            if h + 2 < H:
                plan_wst(h + 2, wq)
        cV_A = cV.n
        cT_A = cT.n
        sWVU = sems["sXQ1"]
        for fc in range(16):
            P.op("sync", lambda e, fc=fc: e.dma_start(
                out=wkus[:, fc * LR:(fc + 1) * LR],
                in_=wvu[fc * 128:(fc + 1) * 128, :]),
                waits=[(cT, cT_A)], inc=sWVU, inc_k=16)
        dS_wvu = sWVU.n
        dS_bst = 0
        for h in range(2):
            plan_wst(H + h, woT)
        for h in range(H):
            for d2 in range(2):
                for j in range(2):
                    P.op("tensor", lambda e, h=h, d2=d2, j=j: e.matmul(
                        psA[(h * 2 + d2) % 2][0:64, :],
                        wkus[:, (2 * h + j) * LR:(2 * h + j + 1) * LR],
                        wst[:, (h % 2) * 2 * D + j * D + d2 * 512:
                               (h % 2) * 2 * D + j * D + (d2 + 1) * 512],
                        start=(j == 0), stop=(j == 1)),
                        waits=[(sWST[h % 2], dS_wst[H + h]), (sWVU, dS_wvu), cV],
                        inc=(cT if j == 1 else None))
                P.op("vector", lambda e, h=h, d2=d2: e.tensor_copy(
                    bstage[0:64, :], psA[(h * 2 + d2) % 2][0:64, :]),
                    waits=[cT, (sBT, dS_bst)], inc=cV)
                cV_bst = cV.n
                P.op("gpsimd", lambda e, h=h, d2=d2: e.dma_start(
                    out=BT[(h % 2) * 64:(h % 2) * 64 + 64,
                           (h // 2) * D + d2 * 512:(h // 2) * D + (d2 + 1) * 512],
                    in_=bstage[0:64, :]), waits=[(cV, cV_bst)], inc=sBT, inc_k=16)
                dS_bst = sBT.n
            wst_free[h % 2] = cT.n
            if h + 2 < H:
                plan_wst(H + h + 2, woT)
        dS_BT_done2 = sBT.n
        P.op("tensor", lambda e: e.matmul(
            psY0[:, 0:1], ones1r[:], scls[:], start=True, stop=True),
            waits=[(sP0, sP0_all), cV], inc=cT)
        P.op("vector", lambda e: e.tensor_copy(sclcol[:], psY0[:, 0:1]),
             waits=[cT], inc=cV)

        cT_pairsum = 0
        for p in range(4):
            for qi in range(NQB):
                for c in range(DC):
                    P.op("tensor", lambda e, p=p, qi=qi, c=c: e.matmul(
                        psA[(p * NQB + qi) % 2][:],
                        Asb[:, c * 512 + p * 128: c * 512 + (p + 1) * 128],
                        xTs[:, c * T + qi * QBS: c * T + (qi + 1) * QBS],
                        start=(c == 0), stop=(c == DC - 1)),
                        waits=[(cV, cV_A), cV],
                        inc=(cT if c == DC - 1 else None))
                par2 = (p * NQB + qi) % 2
                P.op("vector", lambda e, par2=par2: e.tensor_copy(
                    qtf[:], psA[par2][:]), waits=[cT, (sBT, dS_BT_done2)], inc=cV)
                cV_qt = cV.n
                P.op("vector", lambda e, p=p, qi=qi: e.tensor_copy(
                    qth_t[:, p * T + qi * QBS: p * T + (qi + 1) * QBS],
                    qtf[:]), inc=cV)
                P.op("vector", lambda e, p=p, qi=qi: e.tensor_sub(
                    qtl_t[:, p * T + qi * QBS: p * T + (qi + 1) * QBS],
                    qtf[:],
                    qth_t[:, p * T + qi * QBS: p * T + (qi + 1) * QBS]), inc=cV)
                P.op("vector", lambda e: e.tensor_mul(
                    sqp2[:], qtf[:], qtf[:]),
                    waits=[(cT, cT_pairsum)], inc=cV)
                P.op("tensor", lambda e: e.matmul(
                    psU0[0:2, :], onespair[:], sqp2[:], start=True, stop=True),
                    waits=[cV], inc=cT)
                cT_pairsum = cT.n
                P.op("vector", lambda e: e.reduce_max(
                    tmp21[:], psU0[0:2, :], axis=AX.X), waits=[cT], inc=cV)
                P.op("vector", lambda e, p=p: e.tensor_max(
                    qpm[:, p:p + 1], qpm[:, p:p + 1], tmp21[:]),
                    waits=[(cV, cV_consts)], inc=cV)
        cV_qnorm = cV.n
        for p in range(4):
            P.op("gpsimd", lambda e, p=p: e.dma_start(
                out=U8[0:1, 2 * p:2 * p + 2], in_=qpm[:, p:p + 1]),
                waits=[(cV, cV_qnorm)], inc=sU8, inc_k=16)
        dS_U8 = sU8.n
        P.op("vector", lambda e: e.tensor_scalar(
            U8[:], U8[:], row(R_KVM, 1), 0.5, op0=ALU.add, op1=ALU.mult),
            waits=[(sU8, dS_U8)], inc=cV)
        P.op("vector", lambda e: e.tensor_scalar_min(U8[:], U8[:], U_CLAMP), inc=cV)
        cV_U8 = cV.n
        for h in range(H):
            P.op("tensor", lambda e, h=h: e.matmul(
                psY0[:, 0:1], ones1r[:], U8[0:1, h:h + 1], start=True, stop=True),
                waits=[(cV, cV_U8), cV], inc=cT)
            P.op("vector", lambda e, h=h: e.tensor_scalar_mul(
                Ubias[:, h:h + 1], psY0[:, 0:1], -1.0), waits=[cT], inc=cV)
        cV_Ubias = cV.n
        cT_P2_done = cT.n

        dS_BT_done = sBT.n
        P.flush()
        es_p12.close()
        z1cols = es.enter_context(nc.sbuf_tensor("z1cols", [128, DC * 16], F32))

        et = es_att.enter_context(nc.sbuf_tensor("et", [128, 2 * 512], F32, side="right"))
        u_all = es_att.enter_context(nc.sbuf_tensor("u_all", [128, 4 * 512], F32, side="right"))
        xqb = es_att.enter_context(nc.sbuf_tensor("xqb", [128, DC * 512], F32, side="right"))
        ustage = es_att.enter_context(nc.sbuf_tensor("ustage", [64, 512], F32, side="right"))
        kvAh = es_att.enter_context(nc.sbuf_tensor("kvAh", [128, T], BF16, side="right"))
        kvAl = es_att.enter_context(nc.sbuf_tensor("kvAl", [128, T], BF16, side="right"))
        kvBh = es_att.enter_context(nc.sbuf_tensor("kvBh", [128, T], BF16, side="right"))
        kvBl = es_att.enter_context(nc.sbuf_tensor("kvBl", [128, T], BF16, side="right"))
        sqs = es_att.enter_context(nc.sbuf_tensor("sqs", [128, 2 * 512], F32, side="right"))

        P.op("vector", lambda e: e.memset(kvAh[64:128, :], 0.0), waits=[cT], inc=cV)
        P.op("vector", lambda e: e.memset(kvAl[64:128, :], 0.0), inc=cV)
        P.op("vector", lambda e: e.memset(kvBh[0:64, :], 0.0), inc=cV)
        P.op("vector", lambda e: e.memset(kvBl[0:64, :], 0.0), inc=cV)
        P.op("vector", lambda e: e.tensor_copy(kvAh[0:64, :], kv[:]), inc=cV)
        P.op("vector", lambda e: e.tensor_sub(kvAl[0:64, :], kv[:], kvAh[0:64, :]),
             inc=cV)
        cV_kvA = cV.n
        P.op("gpsimd", lambda e: e.dma_start(out=kvBh[64:128, :], in_=kvAh[0:64, :]),
             waits=[(cV, cV_kvA)], inc=sBT, inc_k=16)
        P.op("gpsimd", lambda e: e.dma_start(out=kvBl[64:128, :], in_=kvAl[0:64, :]),
             inc=sBT, inc_k=16)
        sBT_kv = sBT.n

        dS_xqb = [0] * NQB
        xqb_free = [0]
        dS_uall = [0]
        cT_Bproj_prev = [0]
        cV_uall = {}
        cS_exp = {}
        cT_S = {}
        cT_PV = {}
        cT_bcast = {}
        pending_epi = [None]
        pending_ln1 = [None]
        cV_LN1_prev = [0]

        sZ1_done = [0]

        def plan_xqb(qi):
            for c in range(DC):
                P.op("sync", lambda e, qi=qi, c=c: e.dma_start(
                    out=xqb[:, c * 512:(c + 1) * 512],
                    in_=xT[c * 128:(c + 1) * 128, qi * QBS:(qi + 1) * QBS]),
                    waits=[(cV, xqb_free[0]), (sZ1, sZ1_done[0]),
                           (cT, cT_P2_done)],
                    inc=sXQ[0], inc_k=16)
            dS_xqb[qi] = sXQ[0].n

        def plan_epilogue(hq):
            h = hq % H
            P.op("vector", lambda e, h=h: e.reciprocal(
                rcp65[64:65, :], psU[h % 2][64:65, :]),
                waits=[(cT, cT_PV[(hq, KC - 1)]), (cT, cT_bcast.get(hq - 1, 0))],
                inc=cV)
            cV_rcp = cV.n
            P.op("tensor", lambda e, h=h: e.matmul(
                psY[h % 2][0:64, :], ones65[64:65, 0:64], rcp65[64:65, :],
                start=True, stop=True), waits=[(cV, cV_rcp)], inc=cT)
            cT_bcast[hq] = cT.n
            P.op("vector", lambda e, h=h: e.tensor_copy(
                rcp65[0:64, :], psY[h % 2][0:64, :]),
                waits=[(cT, cT_bcast[hq])], inc=cV)
            P.op("vector", lambda e, h=h: e.tensor_mul(
                ustage[:], psU[h % 2][0:64, :], rcp65[0:64, :]),
                waits=[(sUA, dS_uall[0])], inc=cV)
            cV_uall[hq] = cV.n
            P.op("gpsimd", lambda e, h=h: e.dma_start(
                out=u_all[(h % 2) * 64:(h % 2) * 64 + 64,
                          (h // 2) * 512:(h // 2 + 1) * 512],
                in_=ustage[:]),
                waits=[(cV, cV_uall[hq]), (cT, cT_Bproj_prev[0])],
                inc=sUA, inc_k=16)
            dS_uall[0] = sUA.n

        def plan_LN1(qi, cV_r1):
            cT_ss = 0
            for d in range(DC):
                P.op("scalar", lambda e, d=d: e.activation(
                    out=sqs[:, (d % 2) * 512:(d % 2 + 1) * 512],
                    in_=xqb[:, d * 512:(d + 1) * 512], func=AF.Square),
                    waits=[(cV, cV_r1[d]), (cT, cT_ss)], inc=cS)
                cS_sq = cS.n
                P.op("tensor", lambda e, d=d: e.matmul(
                    psL0[0:1, :], ones1c[:], xqb[:, d * 512:(d + 1) * 512],
                    start=(d == 0), stop=(d == DC - 1)),
                    waits=[(cV, cV_r1[d]), (cV, cV_LN1_prev[0]), cS], inc=cT)
                P.op("tensor", lambda e, d=d: e.matmul(
                    psL1[0:1, :], ones1c[:],
                    sqs[:, (d % 2) * 512:(d % 2 + 1) * 512],
                    start=(d == 0), stop=(d == DC - 1)),
                    waits=[(cS, cS_sq), (cV, cV_LN1_prev[0])], inc=cT)
                cT_ss = cT.n
            cT_stats = cT.n
            P.op("vector", lambda e: e.tensor_scalar_mul(
                row(R_MEAN, 512), psL0[0:1, :], 1.0 / D),
                waits=[(cT, cT_stats)], inc=cV)
            P.op("vector", lambda e: e.tensor_scalar_mul(
                row(R_MS, 512), psL1[0:1, :], 1.0 / D), inc=cV)
            P.op("vector", lambda e: e.tensor_mul(
                row(R_VAR, 512), row(R_MEAN, 512), row(R_MEAN, 512)), inc=cV)
            P.op("vector", lambda e: e.tensor_sub(
                row(R_VAR, 512), row(R_MS, 512), row(R_VAR, 512)), inc=cV)
            P.op("vector", lambda e: e.tensor_scalar_add(
                row(R_VAR, 512), row(R_VAR, 512), LN_EPS), inc=cV)
            cV_iv = cV.n
            P.op("scalar", lambda e: e.activation(
                out=row(R_RSTD, 512), in_=row(R_VAR, 512), func=AF.Ln),
                waits=[(cV, cV_iv)], inc=cS)
            P.op("scalar", lambda e: e.activation(
                out=row(R_SR, 512), in_=row(R_RSTD, 512), func=AF.Exp,
                scale=-0.5), inc=cS)
            cS_rstd = cS.n
            P.op("vector", lambda e: e.tensor_mul(
                row(R_MR, 512), row(R_MEAN, 512), row(R_SR, 512)),
                waits=[(cS, cS_rstd)], inc=cV)
            cV_mr = cV.n
            P.op("tensor", lambda e: e.matmul(
                psL0[:], ones1r[:], row(R_SR, 512), start=True, stop=True),
                waits=[(cS, cS_rstd), cV], inc=cT)
            P.op("tensor", lambda e: e.matmul(
                psL1[:], ones1r[:], row(R_MR, 512), start=True, stop=True),
                waits=[(cV, cV_mr)], inc=cT)
            cT_bc2 = cT.n
            for d in range(DC):
                P.op("vector", lambda e, d=d: e.tensor_mul(
                    xqb[:, d * 512:(d + 1) * 512],
                    xqb[:, d * 512:(d + 1) * 512], psL0[:]),
                    waits=[(cT, cT_bc2)], inc=cV)
                P.op("vector", lambda e, d=d: e.tensor_sub(
                    xqb[:, d * 512:(d + 1) * 512],
                    xqb[:, d * 512:(d + 1) * 512], psL1[:]),
                    inc=cV)
                cV_z1w = cV.n
                P.op("gpsimd", lambda e, d=d, qi=qi: e.dma_start(
                    out=z1d[d * 128:(d + 1) * 128, qi * QBS:(qi + 1) * QBS],
                    in_=xqb[:, d * 512:(d + 1) * 512]),
                    waits=[(cV, cV_z1w)], inc=sZ1, inc_k=16)
            if qi == 0:
                for c in range(DC):
                    P.op("vector", lambda e, c=c: e.tensor_copy(
                        z1cols[:, c * 16:(c + 1) * 16],
                        xqb[:, c * 512 + 1: c * 512 + 17]), inc=cV)
            cV_LN1_prev[0] = cV.n
            xqb_free[0] = cV.n
            sZ1_done[0] = sZ1.n
            if qi + 1 < NQB:
                plan_xqb(qi + 1)

        plan_xqb(0)

        for qi in range(NQB):
            for h in range(H):
                hq = qi * H + h

                def plan_S(kc, hq=hq, h=h, qi=qi):
                    par = kc % 2
                    ww = [(cV, cV_Ubias), (sBT, sBT_kv)]
                    prev = cS_exp.get((hq, kc - 2))
                    if prev is None and hq > 0:
                        prev = cS_exp.get((hq - 1, 14 + par))
                    if prev is not None:
                        ww.append((cS, prev))
                    kvh = kvAh if h % 2 == 0 else kvBh
                    kvl = kvAl if h % 2 == 0 else kvBl
                    qs = (h // 2) * T + qi * QBS
                    P.op("tensor", lambda e, kc=kc, par=par, kvh=kvh, qs=qs: e.matmul(
                        psA[par][:], kvh[:, kc * 128:(kc + 1) * 128],
                        qth_t[:, qs:qs + QBS], start=True, stop=False),
                        waits=ww)
                    P.op("tensor", lambda e, kc=kc, par=par, kvh=kvh, qs=qs: e.matmul(
                        psA[par][:], kvh[:, kc * 128:(kc + 1) * 128],
                        qtl_t[:, qs:qs + QBS], start=False, stop=False))
                    P.op("tensor", lambda e, kc=kc, par=par, kvl=kvl, qs=qs: e.matmul(
                        psA[par][:], kvl[:, kc * 128:(kc + 1) * 128],
                        qth_t[:, qs:qs + QBS], start=False, stop=True), inc=cT)
                    cT_S[(hq, kc)] = cT.n

                def plan_exp(kc, hq=hq, h=h):
                    par = kc % 2
                    ww = [(cT, cT_S[(hq, kc)])]
                    prevpv = cT_PV.get((hq, kc - 2))
                    if prevpv is None and hq > 0:
                        prevpv = cT_PV.get((hq - 1, 14 + par))
                    if prevpv is not None:
                        ww.append((cT, prevpv))
                    P.op("scalar", lambda e, kc=kc, par=par, h=h: e.activation(
                        out=et[:, par * 512:(par + 1) * 512], in_=psA[par][:],
                        func=AF.Exp, bias=Ubias[:, h:h + 1], scale=1.0),
                        waits=ww, inc=cS)
                    cS_exp[(hq, kc)] = cS.n

                def plan_PV(kc, hq=hq, h=h):
                    par = kc % 2
                    ww = [(cS, cS_exp[(hq, kc)])]
                    if kc == 0:
                        if hq >= 2:
                            ww.append((cV, cV_uall[hq - 2]))
                        else:
                            ww.append((cV, cV_Ubias))
                    P.op("tensor", lambda e, kc=kc, par=par, h=h: e.matmul(
                        psU[h % 2][:],
                        kvtok[:, kc * 65:(kc + 1) * 65],
                        et[:, par * 512:(par + 1) * 512],
                        start=(kc == 0), stop=(kc == KC - 1)), waits=ww, inc=cT)
                    cT_PV[(hq, kc)] = cT.n

                plan_S(0)
                plan_exp(0)
                plan_S(1)
                plan_exp(1)
                for kc in range(KC):
                    if kc + 2 < KC:
                        plan_S(kc + 2)
                    plan_PV(kc)
                    if kc + 2 < KC:
                        plan_exp(kc + 2)
                    if kc == 1 and pending_epi[0] is not None:
                        plan_epilogue(pending_epi[0])
                        pending_epi[0] = None
                    if kc == 3 and pending_ln1[0] is not None:
                        pqi, pr1 = pending_ln1[0]
                        plan_LN1(pqi, pr1)
                        pending_ln1[0] = None
                if h == H - 1:
                    plan_epilogue(hq)
                else:
                    pending_epi[0] = hq

            cV_r1 = [0] * DC
            for d in range(DC):
                for lc in range(4):
                    P.op("tensor", lambda e, d=d, lc=lc: e.matmul(
                        psY[d % 2][:],
                        BT[:, lc * D + d * 128: lc * D + (d + 1) * 128],
                        u_all[:, lc * 512:(lc + 1) * 512],
                        start=(lc == 0), stop=(lc == 3)),
                        waits=[(sUA, dS_uall[0]), (sBT, dS_BT_done), cV],
                        inc=(cT if lc == 3 else None))
                P.op("vector", lambda e, d=d: e.tensor_add(
                    xqb[:, d * 512:(d + 1) * 512],
                    psY[d % 2][:],
                    xqb[:, d * 512:(d + 1) * 512]),
                    waits=[cT, (sXQ[0], dS_xqb[qi])], inc=cV)
                cV_r1[d] = cV.n
            cT_Bproj_prev[0] = cT.n
            if qi + 1 < NQB:
                pending_ln1[0] = (qi, cV_r1)
            else:
                plan_LN1(qi, cV_r1)
        cV_z1 = cV.n

        P.flush()
        es_att.close()
        es_kv.close()

        hphi = es_ffn.enter_context(nc.sbuf_tensor("hphi", [128, HCN * 512], BF16, side="right"))
        hplo = es_ffn.enter_context(nc.sbuf_tensor("hplo", [128, HCN * 512], BF16, side="right"))
        htmp = es_ffn.enter_context(nc.sbuf_tensor("htmp", [128, 2 * 512], F32, side="right"))
        hgf = es_ffn.enter_context(nc.sbuf_tensor("hgf", [128, 2 * 512], F32, side="right"))
        w1f = es_ffn.enter_context(nc.sbuf_tensor("w1f", [128, 3 * DC * 128], F32, side="right"))
        whl1 = es_ffn.enter_context(nc.sbuf_tensor("whl1", [128, 4 * DC * 256], BF16, side="right"))
        w2f = es_ffn.enter_context(nc.sbuf_tensor("w2f", [128, 3 * 4 * 128], F32, side="right"))
        whl2 = es_ffn.enter_context(nc.sbuf_tensor("whl2", [128, 4 * 4 * 256], BF16, side="right"))
        r2 = es_ffn.enter_context(nc.sbuf_tensor("r2", [128, DC * 512], F32, side="right"))
        u_bc = es_ffn.enter_context(nc.sbuf_tensor("u_bc", [128, 512], F32, side="right"))
        odd4h = es_ffn.enter_context(nc.sbuf_tensor("odd4h", [128, DC * 4], BF16, side="right"))
        odd4l = es_ffn.enter_context(nc.sbuf_tensor("odd4l", [128, DC * 4], BF16, side="right"))
        E4 = es_ffn.enter_context(nc.sbuf_tensor("E4", [4, DC * 128], F32, side="right"))
        sqf = es_ffn.enter_context(nc.sbuf_tensor("sqf", [128, 512 + 32], F32, side="right"))
        z1s = es_ffn.enter_context(nc.sbuf_tensor("z1s", [128, 2 * 4096], BF16, side="right"))
        z1tb = es_ffn.enter_context(nc.sbuf_tensor("z1tb", [128, 2 * DC * 512], F32, side="right"))
        Mt = sqf[:, 512:516]
        u4t = rcp65[0:4, 0:512]
        r2bf = r2[:].bitcast(BF16)

        for c in range(DC):
            P.op("vector", lambda e, c=c: e.tensor_copy(
                odd4h[:, c * 4:(c + 1) * 4],
                z1cols[:, c * 16 + 9: c * 16 + 16: 2]), waits=[(cV, cV_z1)], inc=cV)
            P.op("vector", lambda e, c=c: e.tensor_sub(
                odd4l[:, c * 4:(c + 1) * 4],
                z1cols[:, c * 16 + 9: c * 16 + 16: 2],
                odd4h[:, c * 4:(c + 1) * 4]), inc=cV)
        cV_odd4 = cV.n
        for i in range(4):
            for c in range(DC):
                P.op("tensor", lambda e, i=i, c=c: e.transpose(
                    psA[c % 2][0:1, 0:128],
                    z1cols[:, c * 16 + 8 + 2 * i: c * 16 + 9 + 2 * i], idn[:]),
                    waits=[(cV, cV_z1), cV, cS], inc=cT)
                P.op("vector", lambda e, c=c: e.tensor_scalar(
                    row(P4_MEAN + c * 128, 128), psA[c % 2][0:1, 0:128],
                    scls[:], None, op0=ALU.mult),
                    waits=[cT], inc=cV)
            cV_e4row = cV.n
            for half in range(2):
                d = i * 2 + half
                P.op("gpsimd", lambda e, half=half: e.dma_start(
                    out=Mt, in_=row(P4_MEAN + half * 512, 512)),
                    waits=[(cV, cV_e4row), (cT, cT.n)], inc=sM, inc_k=16)
                dS_M = sM.n
                P.op("tensor", lambda e: e.transpose(
                    psA[0][0:4, 0:128], Mt, idn[:]),
                    waits=[(sM, dS_M), cV], inc=cT)
                P.op("vector", lambda e, d=d: e.tensor_copy(
                    E4[0:4, d * 128:(d + 1) * 128], psA[0][0:4, 0:128]),
                    waits=[cT], inc=cV)
        cV_E4 = cV.n

        dS_w1 = {}
        dS_w1c = {}
        dS_w2 = {}
        dS_w2c = {}
        cG_w1 = {}
        cG_w2 = {}
        w1_free = [0, 0, 0, 0]
        w2_free = [0, 0, 0, 0]
        dO_tbtot = {}
        dS_u4_prev = 0
        cT_ubc_prev = 0
        cV_norm2_prev2 = [0]
        cV_u4t_prev = [cV.n]
        cV_ubc_g = [cV.n]
        sW1C = [sems["sW1C0"], sems["sW1C1"]]
        sW2C = [sems["sW2C0"], sems["sW2C1"]]
        sST = [sems["sST0"], sems["sST1"], sems["sST2"], sems["sST3"]]
        sSU = [sems["sSU0"], sems["sSU1"], sems["sSU2"], sems["sSU3"]]
        st1_done = {}
        st2_done = {}
        sST_tot = [None]
        sSU_tot = [None]

        def plan_w1(tb, hc):
            s = hc % 3
            s4 = hc % 4
            if tb > 0:
                for c in range(DC):
                    P.op("sync", lambda e, hc=hc, c=c, s4=s4: e.dma_start(
                        out=whl1[:, s4 * 2048 + c * 256: s4 * 2048 + (c + 1) * 256],
                        in_=w1sd[hc, c * 128:(c + 1) * 128, :]),
                        waits=[(cT, w1_free[s4]), (sST[s4], sST_tot[0][s4])],
                        inc=sW1[s4], inc_k=16)
                dS_w1[(tb, hc)] = sW1[s4].n
                return
            for c in range(DC):
                P.op("sync", lambda e, hc=hc, c=c, s=s: e.dma_start(
                    out=w1f[:, s * DC * 128 + c * 128: s * DC * 128 + (c + 1) * 128],
                    in_=w1p[hc, c * 128:(c + 1) * 128, :]),
                    waits=[(cG, cG_w1.get((tb * HCN + hc) - 3, 0))],
                    inc=sW1[s], inc_k=16)
            dS_w1[(tb, hc)] = sW1[s].n
            P.op("gpsimd", lambda e, s=s, s4=s4: e.dma_start(
                out=whl1[:, s4 * 2048: s4 * 2048 + 1024],
                in_=w1f[:, s * DC * 128:(s + 1) * DC * 128]),
                waits=[(sW1[s], dS_w1[(tb, hc)]), (cT, w1_free[s4]),
                       (sST[s4], st1_done.get(s4, 0))],
                inc=sW1C[s % 2], inc_k=16)
            dS_w1c[(tb, hc)] = sW1C[s % 2].n
            P.op("gpsimd", lambda e, s=s, s4=s4: e.tensor_sub(
                whl1[:, s4 * 2048 + 1024: s4 * 2048 + 2048],
                w1f[:, s * DC * 128:(s + 1) * DC * 128],
                whl1[:, s4 * 2048: s4 * 2048 + 1024]),
                waits=[(sW1C[s % 2], dS_w1c[(tb, hc)]), (cT, w1_free[s4])], inc=cG)
            cG_w1[tb * HCN + hc] = cG.n

        def plan_w1_store(hc):
            s4 = hc % 4
            cGv = cG_w1[hc]
            for c in range(DC):
                P.op("scalar", lambda e, hc=hc, c=c, s4=s4: e.dma_start(
                    out=w1sd[hc, c * 128:(c + 1) * 128, 0:128],
                    in_=whl1[:, s4 * 2048 + c * 128: s4 * 2048 + (c + 1) * 128]),
                    waits=[(cG, cGv)], inc=sST[s4], inc_k=16)
                P.op("scalar", lambda e, hc=hc, c=c, s4=s4: e.dma_start(
                    out=w1sd[hc, c * 128:(c + 1) * 128, 128:256],
                    in_=whl1[:, s4 * 2048 + 1024 + c * 128:
                             s4 * 2048 + 1024 + (c + 1) * 128]),
                    inc=sST[s4], inc_k=16)
            st1_done[s4] = sST[s4].n

        def plan_w2(tb, d, g):
            gi = d * 8 + g
            s = gi % 3
            s4 = gi % 4
            if tb > 0:
                for k in range(4):
                    P.op("sync", lambda e, d=d, g=g, k=k, s4=s4: e.dma_start(
                        out=whl2[:, s4 * 1024 + k * 256: s4 * 1024 + (k + 1) * 256],
                        in_=w2sd[d, (g * 4 + k) * 128:(g * 4 + k + 1) * 128, :]),
                        waits=[(cT, w2_free[s4]), (sSU[s4], sSU_tot[0][s4])],
                        inc=sW2[s4], inc_k=16)
                dS_w2[(tb, d, g)] = sW2[s4].n
                return
            for k in range(4):
                P.op("sync", lambda e, d=d, g=g, k=k, s=s: e.dma_start(
                    out=w2f[:, s * 4 * 128 + k * 128: s * 4 * 128 + (k + 1) * 128],
                    in_=w2p[d, (g * 4 + k) * 128:(g * 4 + k + 1) * 128, :]),
                    waits=[(cG, cG_w2.get((tb * 64 + d * 8 + g) - 3, 0))],
                    inc=sW2[s], inc_k=16)
            dS_w2[(tb, d, g)] = sW2[s].n
            P.op("gpsimd", lambda e, s=s, s4=s4: e.dma_start(
                out=whl2[:, s4 * 1024: s4 * 1024 + 512],
                in_=w2f[:, s * 512:(s + 1) * 512]),
                waits=[(sW2[s], dS_w2[(tb, d, g)]), (cT, w2_free[s4]),
                       (sSU[s4], st2_done.get(s4, 0))],
                inc=sW2C[s % 2], inc_k=16)
            dS_w2c[(tb, d, g)] = sW2C[s % 2].n
            P.op("gpsimd", lambda e, s=s, s4=s4: e.tensor_sub(
                whl2[:, s4 * 1024 + 512: s4 * 1024 + 1024],
                w2f[:, s * 512:(s + 1) * 512],
                whl2[:, s4 * 1024: s4 * 1024 + 512]),
                waits=[(sW2C[s % 2], dS_w2c[(tb, d, g)]), (cT, w2_free[s4])], inc=cG)
            cG_w2[tb * 64 + d * 8 + g] = cG.n

        def plan_w2_store(d, g):
            gi = d * 8 + g
            s4 = gi % 4
            cGv = cG_w2[gi]
            for k in range(4):
                P.op("scalar", lambda e, d=d, g=g, k=k, s4=s4: e.dma_start(
                    out=w2sd[d, (g * 4 + k) * 128:(g * 4 + k + 1) * 128, 0:128],
                    in_=whl2[:, s4 * 1024 + k * 128: s4 * 1024 + (k + 1) * 128]),
                    waits=[(cG, cGv)], inc=sSU[s4], inc_k=16)
                P.op("scalar", lambda e, d=d, g=g, k=k, s4=s4: e.dma_start(
                    out=w2sd[d, (g * 4 + k) * 128:(g * 4 + k + 1) * 128, 128:256],
                    in_=whl2[:, s4 * 1024 + 512 + k * 128:
                             s4 * 1024 + 512 + (k + 1) * 128]),
                    inc=sSU[s4], inc_k=16)
            st2_done[s4] = sSU[s4].n

        def plan_LN2(tb, cV_r2):
            cT_ss2 = 0
            for d in range(DC):
                P.op("scalar", lambda e, d=d: e.activation(
                    out=sqf[:, 0:512],
                    in_=r2[:, d * 512:(d + 1) * 512], func=AF.Square),
                    waits=[(cV, cV_r2[d]), (cT, cT_ss2)], inc=cS)
                cS_sq2 = cS.n
                P.op("tensor", lambda e, d=d: e.matmul(
                    psL0[0:1, :], ones1c[:], r2[:, d * 512:(d + 1) * 512],
                    start=(d == 0), stop=(d == DC - 1)),
                    waits=[(cV, cV_r2[d]), (cV, cV_norm2_prev2[0]), cS, cV], inc=cT)
                P.op("tensor", lambda e, d=d: e.matmul(
                    psL1[0:1, :], ones1c[:], sqf[:, 0:512],
                    start=(d == 0), stop=(d == DC - 1)),
                    waits=[(cS, cS_sq2), (cV, cV_norm2_prev2[0]), cV], inc=cT)
                cT_ss2 = cT.n
            cT_st2 = cT.n
            P.op("vector", lambda e: e.tensor_scalar_mul(
                row(P4_MEAN, 512), psL0[0:1, :], 1.0 / D),
                waits=[(cT, cT_st2)], inc=cV)
            P.op("vector", lambda e: e.tensor_scalar_mul(
                row(P4_MS, 512), psL1[0:1, :], 1.0 / D), inc=cV)
            P.op("vector", lambda e: e.tensor_mul(
                row(P4_V1, 512), row(P4_MEAN, 512), row(P4_MEAN, 512)), inc=cV)
            P.op("vector", lambda e: e.tensor_sub(
                row(P4_MS, 512), row(P4_MS, 512), row(P4_V1, 512)), inc=cV)
            P.op("vector", lambda e: e.tensor_scalar_add(
                row(P4_MS, 512), row(P4_MS, 512), LN_EPS), inc=cV)
            cV_iv2 = cV.n
            P.op("scalar", lambda e: e.activation(
                out=row(P4_V1, 512), in_=row(P4_MS, 512), func=AF.Ln),
                waits=[(cV, cV_iv2)], inc=cS)
            P.op("scalar", lambda e: e.activation(
                out=row(P4_MS, 512), in_=row(P4_V1, 512), func=AF.Exp,
                scale=-0.5), inc=cS)
            cS_rstd2 = cS.n
            P.op("vector", lambda e: e.tensor_mul(
                row(P4_V1, 512), row(P4_MEAN, 512), row(P4_MS, 512)),
                waits=[(cS, cS_rstd2)], inc=cV)
            cV_mr2 = cV.n
            P.op("tensor", lambda e: e.matmul(
                psL0[:], ones1r[:], row(P4_MS, 512), start=True, stop=True),
                waits=[(cS, cS_rstd2), cV], inc=cT)
            P.op("tensor", lambda e: e.matmul(
                psL1[:], ones1r[:], row(P4_V1, 512), start=True, stop=True),
                waits=[(cV, cV_mr2)], inc=cT)
            cT_bc3 = cT.n
            for d in range(DC):
                P.op("vector", lambda e, d=d: e.tensor_mul(
                    r2[:, d * 512:(d + 1) * 512],
                    r2[:, d * 512:(d + 1) * 512], psL0[:]),
                    waits=[(cT, cT_bc3)], inc=cV)
                P.op("vector", lambda e, d=d: e.tensor_sub(
                    r2[:, d * 512:(d + 1) * 512],
                    r2[:, d * 512:(d + 1) * 512], psL1[:]), inc=cV)
                cV_z2 = cV.n
                P.op("scalar", lambda e, d=d, tb=tb: e.dma_start(
                    out=outT[d * 128:(d + 1) * 128, tb * QBS:(tb + 1) * QBS],
                    in_=r2[:, d * 512:(d + 1) * 512]),
                    waits=[(cV, cV_z2)], inc=dO, inc_k=16)
            cV_norm2_prev2[0] = cV.n
            dO_tbtot[tb] = dO.n

        cT_w1_last = [0]
        pending_ln2 = [None]
        z1tb_free = [0, 0]
        dS_z1tb = {}

        def plan_z1tb(tb):
            par = tb % 2
            for c in range(DC):
                P.op("sync", lambda e, tb=tb, c=c, par=par: e.dma_start(
                    out=z1tb[:, par * DC * 512 + c * 512:
                             par * DC * 512 + (c + 1) * 512],
                    in_=z1d[c * 128:(c + 1) * 128, tb * QBS:(tb + 1) * QBS]),
                    waits=[(cV, z1tb_free[par]), (sZ1, sZ1_done[0])],
                    inc=sZT[par], inc_k=16)
            dS_z1tb[tb] = sZT[par].n

        plan_z1tb(0)
        plan_z1tb(1)

        def z1t(tb, c):
            return z1tb[:, (tb % 2) * DC * 512 + c * 512:
                        (tb % 2) * DC * 512 + (c + 1) * 512]

        for tb in range(NQB):
            for c in range(DC):
                P.op("vector", lambda e, c=c, tb=tb: e.tensor_copy(
                    z1s[:, c * 512:(c + 1) * 512], z1t(tb, c)),
                    waits=[(cT, cT_w1_last[0]), (cV, cV_E4),
                           (sZT[tb % 2], dS_z1tb[tb])], inc=cV)
                P.op("vector", lambda e, c=c, tb=tb: e.tensor_sub(
                    z1s[:, 4096 + c * 512: 4096 + (c + 1) * 512],
                    z1t(tb, c),
                    z1s[:, c * 512:(c + 1) * 512]), inc=cV)
            cV_z1s = cV.n

            for c in range(DC):
                P.op("tensor", lambda e, c=c, tb=tb: e.matmul(
                    psL0[0:4, :],
                    z1cols[:, c * 16 + 1: c * 16 + 9: 2],
                    z1t(tb, c),
                    start=(c == 0), stop=(c == DC - 1)),
                    waits=[(cV, cV_E4), (cV, cV_norm2_prev2[0]),
                           (cV, cV_u4t_prev[0]),
                           (sZT[tb % 2], dS_z1tb[tb])],
                    inc=(cT if c == DC - 1 else None))
            P.op("vector", lambda e: e.tensor_scalar(
                u4t, psL0[0:4, :], sclcol[0:4, :], None, op0=ALU.mult),
                waits=[cT, (sU4, dS_u4_prev)], inc=cV)
            cV_u4 = cV.n
            cV_u4t_prev[0] = cV_u4
            P.op("gpsimd", lambda e: e.dma_start(
                out=row(P4_U4, 2048), in_=u4t),
                waits=[(cV, cV_u4), (cT, cT_ubc_prev)], inc=sU4, inc_k=16)
            dS_u4 = sU4.n
            dS_u4_prev = dS_u4

            if tb == 1 and sST_tot[0] is None:
                sST_tot[0] = [x.n for x in sST]
                sSU_tot[0] = [x.n for x in sSU]
            for pf in range(3 if tb == 0 else 4):
                plan_w1(tb, pf)
            cS_gelu = {}
            cV_sufuse = {}
            cV_hsplit = {}
            cV_ubc = 0
            for hc in range(HCN):
                i = hc // 8
                s = hc % 2
                s4 = hc % 4
                if hc % 8 == 0:
                    P.op("tensor", lambda e, i=i: e.matmul(
                        psL1[:], ones1r[:], row(P4_U4 + i * 512, 512),
                        start=True, stop=True),
                        waits=[(sU4, dS_u4), (cV, cV_norm2_prev2[0]),
                               (cV, cV_ubc_g[0])], inc=cT)
                    cT_ubc = cT.n
                    cT_ubc_prev = cT_ubc
                    P.op("vector", lambda e: e.tensor_copy(u_bc[:], psL1[:]),
                         waits=[(cT, cT_ubc)], inc=cV)
                    cV_ubc = cV.n
                    cV_ubc_g[0] = cV_ubc
                for c in range(DC):
                    if tb == 0:
                        hi1 = s4 * 2048 + c * 128
                        lo1 = s4 * 2048 + 1024 + c * 128
                        wwt = (cG, cG_w1[tb * HCN + hc])
                    else:
                        hi1 = s4 * 2048 + c * 256
                        lo1 = s4 * 2048 + c * 256 + 128
                        wwt = (sW1[s4], dS_w1[(tb, hc)])
                    ww = [wwt, (cV, cV_z1s), (cV, cV_ubc)]
                    if c == 0:
                        ww.append((cV, cV_sufuse.get(hc - 4, 0)))
                    P.op("tensor", lambda e, c=c, s4=s4, hi1=hi1: e.matmul(
                        psP[s4][:],
                        whl1[:, hi1: hi1 + 128],
                        z1s[:, c * 512:(c + 1) * 512],
                        start=(c == 0), stop=False), waits=ww)
                    P.op("tensor", lambda e, c=c, s4=s4, hi1=hi1: e.matmul(
                        psP[s4][:],
                        whl1[:, hi1: hi1 + 128],
                        z1s[:, 4096 + c * 512: 4096 + (c + 1) * 512],
                        start=False, stop=False))
                    P.op("tensor", lambda e, c=c, s4=s4, lo1=lo1: e.matmul(
                        psP[s4][:],
                        whl1[:, lo1: lo1 + 128],
                        z1s[:, c * 512:(c + 1) * 512],
                        start=False, stop=(c == DC - 1)),
                        inc=(cT if c == DC - 1 else None))
                cT_w1 = cT.n
                w1_free[s4] = cT.n
                if tb == 0:
                    plan_w1_store(hc)
                adv1 = 3 if tb == 0 else 4
                if hc + adv1 < HCN:
                    plan_w1(tb, hc + adv1)
                if hc == 2 and pending_ln2[0] is not None:
                    ptb, pr2 = pending_ln2[0]
                    plan_LN2(ptb, pr2)
                    pending_ln2[0] = None
                P.op("vector", lambda e, hc=hc, s=s, s4=s4, i=i: e.scalar_tensor_tensor(
                    htmp[:, s * 512:(s + 1) * 512], u_bc[:],
                    z1cols[:, (hc % 8) * 16 + 2 * i: (hc % 8) * 16 + 2 * i + 1],
                    psP[s4][:], op0=ALU.mult, op1=ALU.add),
                    waits=[(cT, cT_w1), (cS, cS_gelu.get(hc - 2, 0))], inc=cV)
                cV_sufuse[hc] = cV.n
                P.op("scalar", lambda e, hc=hc, s=s: e.activation(
                    out=hgf[:, s * 512:(s + 1) * 512],
                    in_=htmp[:, s * 512:(s + 1) * 512],
                    func=AF.Gelu, bias=b1s[:, hc:hc + 1]),
                    waits=[(cV, cV_sufuse[hc]), (cV, cV_hsplit.get(hc - 2, 0))],
                    inc=cS)
                cS_gelu[hc] = cS.n
                P.op("vector", lambda e, hc=hc, s=s: e.tensor_copy(
                    hphi[:, hc * 512:(hc + 1) * 512],
                    hgf[:, s * 512:(s + 1) * 512]),
                    waits=[(cS, cS_gelu[hc])], inc=cV)
                P.op("vector", lambda e, hc=hc, s=s: e.tensor_sub(
                    hplo[:, hc * 512:(hc + 1) * 512],
                    hgf[:, s * 512:(s + 1) * 512],
                    hphi[:, hc * 512:(hc + 1) * 512]), inc=cV)
                cV_hsplit[hc] = cV.n
            cT_w1_last[0] = cT.n

            cV_T1ev = {}
            for j in range(4):
                for c in range(DC):
                    hb = (j * 8 + c) * 512
                    ww = [(cV, cV_hsplit[j * 8 + c]), (cV, cV_odd4)]
                    if c == 0:
                        ww.append((cV, cV_T1ev.get(j - 2, 0)))
                        ww.append((cV, cV_sufuse[30 + (j % 2)]))
                    P.op("tensor", lambda e, c=c, hb=hb, j=j: e.matmul(
                        psA[j % 2][0:4, :], odd4h[:, c * 4:(c + 1) * 4],
                        hphi[:, hb:hb + 512],
                        start=(c == 0), stop=False), waits=ww)
                    P.op("tensor", lambda e, c=c, hb=hb, j=j: e.matmul(
                        psA[j % 2][0:4, :], odd4h[:, c * 4:(c + 1) * 4],
                        hplo[:, hb:hb + 512], start=False, stop=False))
                    P.op("tensor", lambda e, c=c, hb=hb, j=j: e.matmul(
                        psA[j % 2][0:4, :], odd4l[:, c * 4:(c + 1) * 4],
                        hphi[:, hb:hb + 512], start=False, stop=(c == DC - 1)),
                        inc=(cT if c == DC - 1 else None))
                cT_T1j = cT.n
                P.op("vector", lambda e, j=j: e.tensor_copy(
                    r2[0:4, j * 512:(j + 1) * 512], psA[j % 2][0:4, :]),
                    waits=[(cT, cT_T1j), (dO, dO_tbtot.get(tb - 1, 0))], inc=cV)
                cV_T1ev[j] = cV.n
            cV_T1 = cV.n
            for i in range(4):
                sl = r2[i:i + 1, 0:2048]
                srcap = bass.AP(tensor=sl.tensor, offset=sl.offset,
                                ap=[[sl.ap[0][0], 1], [512, 4], [1, 512]])
                P.op("gpsimd", lambda e, i=i, srcap=srcap: e.dma_start(
                    out=r2[0:4, 2048 + i * 512: 2048 + (i + 1) * 512], in_=srcap),
                    waits=[(cV, cV_T1)], inc=sT1, inc_k=16)
            dS_T1 = sT1.n

            cV_r2 = [0] * DC
            for pf in range(3 if tb == 0 else 4):
                plan_w2(tb, 0, pf)
            for d in range(DC):
                for g in range(8):
                    for k in range(4):
                        hc = g * 4 + k
                        s4w = (d * 8 + g) % 4
                        if tb == 0:
                            hi2 = s4w * 1024 + k * 128
                            lo2 = s4w * 1024 + 512 + k * 128
                            wwt = (cG, cG_w2[tb * 64 + d * 8 + g])
                        else:
                            hi2 = s4w * 1024 + k * 256
                            lo2 = s4w * 1024 + k * 256 + 128
                            wwt = (sW2[s4w], dS_w2[(tb, d, g)])
                        ww = [wwt]
                        if g == 0 and k == 0:
                            if d >= 4:
                                ww.append((cV, cV_r2[d - 4]))
                            elif d >= 2:
                                ww.append((cV, cV_T1ev[d]))
                            else:
                                ww.append((cV, cV_sufuse[28 + d]))
                        st = (g == 0 and k == 0)
                        P.op("tensor", lambda e, d=d, hi2=hi2, hc=hc, st=st: e.matmul(
                            psP[d % 4][:],
                            whl2[:, hi2: hi2 + 128],
                            hphi[:, hc * 512:(hc + 1) * 512],
                            start=st, stop=False), waits=ww)
                        P.op("tensor", lambda e, d=d, hi2=hi2, hc=hc: e.matmul(
                            psP[d % 4][:],
                            whl2[:, hi2: hi2 + 128],
                            hplo[:, hc * 512:(hc + 1) * 512],
                            start=False, stop=False))
                        P.op("tensor", lambda e, d=d, lo2=lo2, hc=hc, k=k: e.matmul(
                            psP[d % 4][:],
                            whl2[:, lo2: lo2 + 128],
                            hphi[:, hc * 512:(hc + 1) * 512],
                            start=False, stop=False),
                            inc=(cT if k == 3 else None))
                    w2_free[(d * 8 + g) % 4] = cT.n
                    if tb == 0:
                        plan_w2_store(d, g)
                    sidx = d * 8 + g
                    adv2 = 3 if tb == 0 else 4
                    if sidx + adv2 <= 63:
                        nd, ng = divmod(sidx + adv2, 8)
                        plan_w2(tb, nd, ng)
                P.op("tensor", lambda e, d=d: e.matmul(
                    psP[d % 4][:],
                    E4[0:4, d * 128:(d + 1) * 128],
                    r2[0:4, 2048 + (d // 2) * 512: 2048 + (d // 2 + 1) * 512],
                    start=False, stop=True),
                    waits=[(sT1, dS_T1), (cV, cV_E4)], inc=cT)
                cT_w2 = cT.n
                P.op("vector", lambda e, d=d, tb=tb: e.scalar_tensor_tensor(
                    r2[:, d * 512:(d + 1) * 512], psP[d % 4][:],
                    b2s[:, d:d + 1], z1t(tb, d),
                    op0=ALU.add, op1=ALU.add),
                    waits=[(cT, cT_w2), (sT1, dS_T1),
                           (dO, dO_tbtot.get(tb - 1, 0))], inc=cV)
                cV_r2[d] = cV.n

            z1tb_free[tb % 2] = cV.n
            if tb + 2 < NQB:
                plan_z1tb(tb + 2)
            if tb + 1 < NQB:
                pending_ln2[0] = (tb, cV_r2)
            else:
                plan_LN2(tb, cV_r2)



        P.op("sync", lambda e: e.nop(), waits=[(dO, dO.n)])
        P.flush()
        es_ffn.close()
    es.close()
    return nc


_NC_CACHE = None


def kernel(L0_attn_x, L0_attn_Wq, L0_attn_Wkv_down, L0_attn_Wk_up, L0_attn_Wv_up,
           L0_attn_Wo, L0_ffn_W1, L0_ffn_b1, L0_ffn_W2, L0_ffn_b2, L0_ffn_scale):
    global _NC_CACHE
    x = np.asarray(L0_attn_x, dtype=np.float32)
    B = x.shape[0]
    wq_h = np.ascontiguousarray(np.asarray(L0_attn_Wq, np.float32))
    wkvT_h = np.ascontiguousarray(np.asarray(L0_attn_Wkv_down, np.float32).T)
    wku_h = np.ascontiguousarray(np.asarray(L0_attn_Wk_up, np.float32))
    wvu_h = np.ascontiguousarray(np.asarray(L0_attn_Wv_up, np.float32))
    woT_h = np.ascontiguousarray(np.asarray(L0_attn_Wo, np.float32).T)
    w1T = np.asarray(L0_ffn_W1, np.float32).T
    w1p_h = np.ascontiguousarray(w1T.reshape(D, HCN, 128).transpose(1, 0, 2))
    w2T = np.asarray(L0_ffn_W2, np.float32).T
    w2p_h = np.ascontiguousarray(w2T.reshape(FF, DC, 128).transpose(1, 0, 2))
    b1c_h = np.ascontiguousarray(np.asarray(L0_ffn_b1, np.float32).reshape(HCN, 128).T)
    b2c_h = np.ascontiguousarray(np.asarray(L0_ffn_b2, np.float32).reshape(DC, 128).T)
    scl_h = np.ascontiguousarray(np.asarray(L0_ffn_scale, np.float32).reshape(1, 1))

    if _NC_CACHE is None:
        _NC_CACHE = build_nc()
    nc = _NC_CACHE

    in_maps = [{
        "xT": np.ascontiguousarray(x[b].T),
        "wq": wq_h, "wkvT": wkvT_h, "wku": wku_h, "wvu": wvu_h, "woT": woT_h,
        "w1p": w1p_h, "w2p": w2p_h, "b1c": b1c_h, "b2c": b2c_h, "scl": scl_h,
    } for b in range(B)]
    res = run_bass_kernel_spmd(nc, in_maps, core_ids=list(range(B)))
    out = np.stack([np.asarray(r["outT"]).T for r in res.results], axis=0)
    return np.ascontiguousarray(out, dtype=np.float32)
